# version 09
# baseline (speedup 1.0000x reference)
import math

import numpy as np

import concourse.bass as bass
import concourse.mybir as mybir
from concourse.bass_utils import run_bass_kernel_spmd

F32 = mybir.dt.float32
BF16 = mybir.dt.bfloat16
AF = mybir.ActivationFunctionType
ALU = mybir.AluOpType
AX = mybir.AxisListType

NEG_BIG = -1.0e30


class Cfg:
    def __init__(self, D=4096, S=4096, H=32, R=64, E=16):
        self.D = D
        self.S = S
        self.B = 2
        self.H = H
        self.HD = 128
        self.NC = 8
        self.G = 4
        self.R = R
        self.E = E
        assert D == H * 128
        self.TOK = S // self.G
        self.TT = self.TOK // 128
        self.DCH = D // 128
        self.QT = S // 128
        self.HL = H // self.G
        self.EB1 = D // 512
        self.KCH = 1024
        self.RC = (E * R) // 128
        self.CLIP = 8.0
        self.EPS = 1.0e-5
        self.SCALING = 32.0 / 64.0
        self.ISQ = 1.0 / math.sqrt(128.0)
        self.R0W = 128 + min(self.KCH, S)
        assert self.HL % 2 == 0 and self.DCH % 8 == 0
        self.TB = min(512, self.TOK)
        assert self.TOK % 128 == 0 and self.R == 64


def alibi_slopes(h, bias_max=8):
    p2 = 2 ** math.ceil(math.log2(h))
    m = [(i + 1) * (bias_max / p2) for i in range(p2)]
    s = [1.0 / (2.0 ** mi) for mi in m]
    if p2 != h:
        s = (s[1::2] + s[::2])[:h]
    return s


class Chan:
    def __init__(self, nc, name):
        self.h = nc.alloc_semaphore(name=name)
        self.n = 0


class Prog:
    ENGINES = ("tensor", "vector", "scalar", "gpsimd", "sync")

    def __init__(self, nc):
        self.nc = nc
        self.streams = {e: [] for e in self.ENGINES}
        self._chans = {}
        self._prev_tag = {}

    def chan(self, name):
        if name not in self._chans:
            self._chans[name] = Chan(self.nc, name)
        return self._chans[name]

    MAX_SIG = 1
    CHAINED = ("vector", "scalar", "gpsimd")

    def step(self, eng, fn, wait=(), sig=(), isdma=False):
        waits = []
        for w in wait:
            ch, v = (w if isinstance(w, tuple) else (w, None))
            v = ch.n if v is None else v
            if v > 0:
                waits.append((ch.h, v))
        if eng in self.CHAINED and not isdma:
            prev = self._prev_tag.get(eng)
            if prev is not None:
                waits.append(prev)
        sigs = []
        for s in sig:
            ch, inc = (s if isinstance(s, tuple) else (s, 1))
            ch.n += inc
            sigs.append((ch.h, inc))
        if eng in self.CHAINED and not isdma:
            oc = self.chan("_ord_" + eng)
            oc.n += 1
            self._prev_tag[eng] = (oc.h, oc.n)
            self.streams[eng].append((fn, waits, [(oc.h, 1)]))
            for extra in sigs:
                self.streams[eng].append(("NOP", [(oc.h, oc.n)], [extra]))
        else:
            self.streams[eng].append((fn, waits, sigs[:self.MAX_SIG]))
            for extra in sigs[self.MAX_SIG:]:
                self.streams[eng].append(("DNOP", [], [extra]))

    def replay(self, blk):
        def mk(eng):
            def body(e):
                for fn, waits, sigs in self.streams[eng]:
                    if fn in ("NOP", "DNOP"):
                        if fn == "DNOP":
                            e.drain()
                        for sem, v in waits:
                            e.wait_ge(sem, v)
                        inst = e.nop()
                        for sem, inc in sigs:
                            inst.then_inc(sem, inc)
                        continue
                    for sem, v in waits:
                        e.wait_ge(sem, v)
                    inst = fn(e) if fn is not None else None
                    for sem, inc in sigs:
                        inst.then_inc(sem, inc)
            return body
        blk.tensor(mk("tensor"))
        blk.vector(mk("vector"))
        blk.scalar(mk("scalar"))
        blk.gpsimd(mk("gpsimd"))
        blk.sync(mk("sync"))

    def counts(self):
        return {e: len(s) for e, s in self.streams.items()}


class Carver:
    def __init__(self, arena_ap, total_bytes):
        self.a = arena_ap
        self.total = total_bytes
        self.off = 0
        self.marks = []
        self.peak = 0

    def take(self, n_elems, dtype):
        esz = 4 if dtype == F32 else 2
        al = (self.off + 63) // 64 * 64
        start = al // 2
        nb = n_elems * esz
        self.off = al + nb
        self.peak = max(self.peak, self.off)
        assert self.off <= self.total, f"SBUF arena overflow: {self.off} > {self.total}"
        sl = self.a[:, start:start + nb // 2]
        if dtype == F32:
            sl = sl.bitcast(F32)
        return sl

    def push(self):
        self.marks.append(self.off)

    def pop(self):
        self.off = self.marks.pop()


def build(cfg: Cfg, upto=99):
    c = cfg
    nc = bass.Bass("TRN2", target_bir_lowering=False, debug=False)

    x_in = nc.dram_tensor("x", [c.TOK, c.D], F32, kind="ExternalInput").ap()
    wqkvT = nc.dram_tensor("wqkvT", [c.D, 3 * c.D], F32, kind="ExternalInput").ap()
    woutT = nc.dram_tensor("woutT", [c.D, c.D], F32, kind="ExternalInput").ap()
    basewT = nc.dram_tensor("basewT", [c.D, c.D], F32, kind="ExternalInput").ap()
    at_in = nc.dram_tensor("AT", [c.D, c.R], F32, kind="ExternalInput").ap()
    rwt_in = nc.dram_tensor("RWT", [c.D, c.E], F32, kind="ExternalInput").ap()
    bstk_in = nc.dram_tensor("Bstk", [c.E * c.R, c.D], F32, kind="ExternalInput").ap()
    baseb_in = nc.dram_tensor("baseb", [1, c.D], F32, kind="ExternalInput").ap()
    qlnw_in = nc.dram_tensor("qlnw", [128, c.DCH], F32, kind="ExternalInput").ap()
    qlnb_in = nc.dram_tensor("qlnb", [128, c.DCH], F32, kind="ExternalInput").ap()
    klnw_in = nc.dram_tensor("klnw", [128, c.DCH], F32, kind="ExternalInput").ap()
    klnb_in = nc.dram_tensor("klnb", [128, c.DCH], F32, kind="ExternalInput").ap()
    slop_in = nc.dram_tensor("slopes", [1, c.HL], F32, kind="ExternalInput").ap()

    y_out = nc.dram_tensor("out", [c.TOK, c.D], F32, kind="ExternalOutput").ap()

    qstage = nc.dram_tensor("qstage", [c.TOK, c.D], BF16, kind="Internal").ap()
    kstage = nc.dram_tensor("kstage", [c.TOK, c.D], BF16, kind="Internal").ap()
    x1d = nc.dram_tensor("x1d", [c.TOK, c.D], F32, kind="Internal").ap()
    combT_d = nc.dram_tensor("combT_d", [c.E, c.TOK], F32, kind="Internal").ap()
    agkq_in = nc.dram_tensor("agkq_in", [2 * c.D, c.TOK], BF16, kind="Internal").ap()
    agk_out = nc.dram_tensor("agk_out", [c.G * c.D, c.TOK], BF16, kind="Internal").ap()
    agq_out = nc.dram_tensor("agq_out", [c.G * c.D, c.TOK], BF16, kind="Internal").ap()
    agv_in = nc.dram_tensor("agv_in", [c.TOK, c.D], BF16, kind="Internal").ap()
    agv_out = nc.dram_tensor("agv_out", [c.S, c.D], BF16, kind="Internal").ap()
    agat_in = nc.dram_tensor("agat_in", [c.D, c.TOK], BF16, kind="Internal").ap()
    agat_out = nc.dram_tensor("agat_out", [c.G * c.D, c.TOK], BF16, kind="Internal").ap()

    RG = [[0, 1, 2, 3], [4, 5, 6, 7]]

    with (
        nc.sbuf_tensor("arena", [128, 105472], BF16) as arena_t,
        nc.psum_tensor("ps_s0", [128, 1024], F32) as ps_s0_t,
        nc.psum_tensor("ps_s1", [128, 1024], F32) as ps_s1_t,
        nc.psum_tensor("ps_o", [128, 512], F32) as ps_o_t,
        nc.psum_tensor("ps_t", [128, 2048], BF16) as ps_t_t,
        nc.psum_tensor("ps_w", [128, 512], F32) as ps_w_t,
        nc.Block() as blk,
    ):
        arena = arena_t.ap()
        ps_s0 = ps_s0_t.ap()
        ps_s1 = ps_s1_t.ap()
        ps_o = ps_o_t.ap()
        ps_t = ps_t_t.ap()
        ps_w = ps_w_t.ap()
        ps_mm = [ps_s0[:, 0:512], ps_s0[:, 512:1024],
                 ps_s1[:, 0:512], ps_s1[:, 512:1024]]

        P = Prog(nc)
        cv = Carver(arena, 156672)
        SCR_BASE = 156672
        scr = Carver(arena, 210944)
        scr.off = SCR_BASE

        id_bf = cv.take(128, BF16)
        id_f32 = cv.take(128, F32)
        ones_f32 = cv.take(128, F32)
        sclrow = cv.take(64, F32)
        onesrow_bf = cv.take(128, BF16)
        r0n = cv.take(c.R0W, F32)
        qlnw = cv.take(c.DCH, F32)
        qlnb = cv.take(c.DCH, F32)
        klnw = cv.take(c.DCH, F32)
        klnb = cv.take(c.DCH, F32)
        slope_col = cv.take(max(c.HL, 8), F32)
        sloprow = cv.take(max(c.HL, 8), F32)
        statw = cv.take(6 * (c.D // 512), F32)
        stat2 = cv.take(8, F32)
        tcol0 = cv.take(16, F32)
        tcol1 = cv.take(8, F32)
        rlrow = cv.take(128, F32)
        rlb = cv.take(128, F32)
        epscol = cv.take(1, F32)
        nhalf = cv.take(1, F32)
        selm = cv.take(128, F32)

        mean_c = stat2[:, 0:1]
        var_c = stat2[:, 1:2]
        rstd_c = stat2[:, 2:3]
        negm_c = stat2[:, 3:4]

        _rank_cache = {}
        _fill_cache = {}

        def fillreg(e, v):
            import struct
            k = (id(e), v)
            if k not in _fill_cache:
                _fill_cache[k] = e.to_reg(
                    struct.unpack('<i', struct.pack('<f', v))[0])
            return _fill_cache[k]

        def rankof(e):
            k = id(e)
            if k not in _rank_cache:
                _rank_cache[k] = e.partition_id() % c.G
            return _rank_cache[k]

        def dma(eng, out, in_, wait=(), sig=()):
            P.step(eng, lambda e, o=out, i=in_: e.dma_start(out=o, in_=i),
                   wait=wait, sig=sig, isdma=True)

        ld = P.chan("ld")
        cst = P.chan("cst")
        dma("sync", qlnw, qlnw_in, sig=[(ld, 16)])
        dma("sync", qlnb, qlnb_in, sig=[(ld, 16)])
        dma("sync", klnw, klnw_in, sig=[(ld, 16)])
        dma("sync", klnb, klnb_in, sig=[(ld, 16)])
        dma("sync", sloprow[0:1, 0:c.HL], slop_in, sig=[(ld, 16)])
        ldg = P.chan("ldg")

        P.step("gpsimd", lambda e: e.memset(id_bf[:, :], 1.0))
        P.step("gpsimd", lambda e: e.affine_select(
            id_bf[:, :], id_bf[:, :], [[1, 128]], ALU.is_equal, 0.0,
            base=0, channel_multiplier=-1), sig=[(cst, 1)])
        P.step("gpsimd", lambda e: e.memset(id_f32[:, :], 1.0))
        P.step("gpsimd", lambda e: e.affine_select(
            id_f32[:, :], id_f32[:, :], [[1, 128]], ALU.is_equal, 0.0,
            base=0, channel_multiplier=-1), sig=[(cst, 1)])
        P.step("gpsimd", lambda e: e.memset(ones_f32[0:1, :], 1.0), sig=[(cst, 1)])
        P.step("gpsimd", lambda e: e.memset(sclrow[0:1, :], c.SCALING), sig=[(cst, 1)])
        P.step("gpsimd", lambda e: e.memset(onesrow_bf[0:1, :], 1.0), sig=[(cst, 1)])
        P.step("gpsimd", lambda e: e.memset(epscol, c.EPS), sig=[(cst, 1)])
        P.step("gpsimd", lambda e: e.memset(nhalf, -0.5), sig=[(cst, 1)])
        P.step("gpsimd", lambda e: e.memset(selm[0:2, :], c.SCALING), sig=[(cst, 1)])
        P.step("gpsimd", lambda e: e.affine_select(
            selm[0:2, :], selm[0:2, :], [[1, 128]], ALU.is_ge,
            fillreg(e, 0.0), base=0, channel_multiplier=-64), sig=[(cst, 1)])
        P.step("gpsimd", lambda e: e.affine_select(
            selm[0:2, :], selm[0:2, :], [[-1, 128]], ALU.is_ge,
            fillreg(e, 0.0), base=63, channel_multiplier=64), sig=[(cst, 1)])
        P.step("gpsimd", lambda e: e.iota(
            r0n[:, :], [[1, c.R0W]], base=-127, channel_multiplier=-1,
            allow_small_or_imprecise_dtypes=True),
            sig=[(cst, 1)])
        P.step("vector", lambda e: e.tensor_scalar_mul(qlnw, qlnw, c.ISQ),
               wait=[ld], sig=[(cst, 1)])
        P.step("vector", lambda e: e.tensor_scalar_mul(qlnb, qlnb, c.ISQ),
               wait=[ld], sig=[(cst, 1)])
        P.step("tensor", lambda e: e.matmul(
            ps_w[:, 0:c.HL], ones_f32[0:1, :], sloprow[0:1, 0:c.HL],
            start=True, stop=True), wait=[ld, cst], sig=[(cst, 1)])
        P.step("vector", lambda e: e.tensor_copy(slope_col[:, 0:c.HL], ps_w[:, 0:c.HL]),
               wait=[cst], sig=[(cst, 1)])
        CST = cst.n
        LDF = ld.n

        tp_state = {"par": 0, "last": [None, None]}

        def tp_half():
            h = tp_state["par"] % 2
            tp_state["par"] += 1
            return h

        def emit_ln_transpose(src_dram, hT, gate):
            htile = scr.take(c.D, BF16)
            xt = [scr.take(c.D, F32), scr.take(c.D, F32)]
            ch_x = [P.chan(f"lt_x{i}") for i in range(2)]
            ch_xf = [P.chan(f"lt_xf{i}") for i in range(2)]
            ch_st = P.chan("lt_st")
            ch_h = P.chan("lt_h")
            ch_tp = P.chan("lt_tp")
            ch_ev = P.chan("lt_ev")
            ngrp = c.DCH // 8
            for tt in range(c.TT):
                sl = tt % 2
                src = src_dram[tt * 128:(tt + 1) * 128, :]
                w = list(gate) + ([ch_xf[sl]] if ch_xf[sl].n > 0 else [])
                dma("sync", xt[sl], src, wait=w, sig=[(ch_x[sl], 16)])
                nseg = c.D // 512
                for sgi in range(nseg):
                    P.step("vector", lambda e, s=sgi, xi=xt[sl]: e.bn_stats(
                        statw[:, 6 * s:6 * s + 6], xi[:, 512 * s:512 * (s + 1)]),
                        wait=[ch_x[sl]] if sgi == 0 else ())
                P.step("vector", lambda e, n=nseg: e.bn_aggr(
                    stat2[:, 0:2], statw[:, 0:6 * n]))
                P.step("vector", lambda e: e.tensor_scalar_mul(negm_c, mean_c, -1.0),
                       sig=[(ch_st, 1)])
                P.step("scalar", lambda e: e.activation(
                    rstd_c, var_c, AF.Ln, bias=epscol), wait=[ch_st])
                P.step("scalar", lambda e: e.activation(
                    rstd_c, rstd_c, AF.Exp, scale=nhalf), sig=[(ch_st, 1)])
                P.step("vector", lambda e, xi=xt[sl]: e.tensor_scalar(
                    htile, xi, negm_c, rstd_c, op0=ALU.add, op1=ALU.mult),
                    wait=[ch_st], sig=[(ch_h, 1), (ch_xf[sl], 1)])
                for g in range(ngrp):
                    hf = tp_half()
                    war = tp_state["last"][hf]
                    for j in range(8):
                        cc = g * 8 + j
                        P.step("tensor", lambda e, ci=cc, j2=j, h2=hf: e.transpose(
                            ps_t[:, h2 * 1024 + 128 * j2:h2 * 1024 + 128 * (j2 + 1)],
                            htile[:, 128 * ci:128 * (ci + 1)], id_bf[:, :]),
                            wait=([ch_h, (cst, CST)] + ([war] if war else [])) if j == 0 else (),
                            sig=[(ch_tp, 1)] if j == 7 else ())
                    dst = hT.rearrange("p (ch t) -> p ch t", t=c.TOK)
                    P.step("vector", lambda e, g2=g, h2=hf, d=dst, t0=tt: e.tensor_copy(
                        d[:, g2 * 8:g2 * 8 + 8, t0 * 128:(t0 + 1) * 128],
                        ps_t[:, h2 * 1024:(h2 + 1) * 1024].rearrange(
                            "p (b t) -> p b t", t=128)),
                        wait=[ch_tp], sig=[(ch_ev, 1)])
                    tp_state["last"][hf] = (ch_ev, ch_ev.n)
            return ch_ev

        def emit_qk_ln(stage_dram, wcol, bcol_g, row0, gate):
            ktile = scr.take(c.D, BF16)
            ntile = scr.take(c.D, BF16)
            otile = scr.take(c.DCH * 128, BF16)
            ch_x = P.chan("qk_x")
            ch_st = P.chan("qk_st")
            ch_n = P.chan("qk_n")
            ch_tp = P.chan("qk_tp")
            ch_ev = P.chan("qk_ev")
            ch_sd = P.chan("qk_sd")
            ngrp = c.DCH // 8
            for tt in range(c.TT):
                src = stage_dram[tt * 128:(tt + 1) * 128, :]
                dma("sync", ktile, src,
                    wait=list(gate) + ([ch_n] if ch_n.n > 0 else []),
                    sig=[(ch_x, 16)])
                nseg = c.D // 512
                for sgi in range(nseg):
                    P.step("vector", lambda e, s=sgi: e.bn_stats(
                        statw[:, 6 * s:6 * s + 6], ktile[:, 512 * s:512 * (s + 1)]),
                        wait=[ch_x] if sgi == 0 else ())
                P.step("vector", lambda e, n=nseg: e.bn_aggr(
                    stat2[:, 0:2], statw[:, 0:6 * n]))
                P.step("vector", lambda e: e.tensor_scalar_mul(negm_c, mean_c, -1.0),
                       sig=[(ch_st, 1)])
                P.step("scalar", lambda e: e.activation(
                    rstd_c, var_c, AF.Ln, bias=epscol), wait=[ch_st])
                P.step("scalar", lambda e: e.activation(
                    rstd_c, rstd_c, AF.Exp, scale=nhalf), sig=[(ch_st, 1)])
                P.step("vector", lambda e: e.tensor_scalar(
                    ntile, ktile, negm_c, rstd_c, op0=ALU.add, op1=ALU.mult),
                    wait=[ch_st, (ch_tp, ch_tp.n)], sig=[(ch_n, 1)])
                for g in range(ngrp):
                    hf = tp_half()
                    war = tp_state["last"][hf]
                    for j in range(8):
                        cc = g * 8 + j
                        P.step("tensor", lambda e, ci=cc, j2=j, h2=hf: e.transpose(
                            ps_t[:, h2 * 1024 + 128 * j2:h2 * 1024 + 128 * (j2 + 1)],
                            ntile[:, 128 * ci:128 * (ci + 1)], id_bf[:, :]),
                            wait=([ch_n] + ([war] if war else [])) if j == 0 else (),
                            sig=[(ch_tp, 1)] if j == 7 else ())
                    for j in range(8):
                        cc = g * 8 + j
                        P.step("vector", lambda e, ci=cc, j2=j, h2=hf: e.tensor_scalar(
                            otile[:, 128 * ci:128 * (ci + 1)],
                            ps_t[:, h2 * 1024 + 128 * j2:h2 * 1024 + 128 * (j2 + 1)],
                            wcol[:, ci:ci + 1], bcol_g[:, ci:ci + 1],
                            op0=ALU.mult, op1=ALU.add),
                            wait=([ch_tp] + ([(ch_sd, ch_sd.n)] if g == 0 else []))
                                 if j == 0 else (),
                            sig=[(ch_ev, 1)] if j == 7 else ())
                    tp_state["last"][hf] = (ch_ev, ch_ev.n)
                dstv = agkq_in.rearrange("(r p) t -> p r t", p=128)
                P.step("sync", lambda e, t0=tt, d=dstv: e.dma_start(
                    out=d[:, row0 // 128:row0 // 128 + c.DCH,
                          t0 * 128:(t0 + 1) * 128],
                    in_=otile.rearrange("p (r t) -> p r t", t=128)),
                    wait=[ch_ev], sig=[(ch_sd, 16)])
            return ch_sd

        cv.push()
        hT = cv.take(c.DCH * c.TOK, BF16)
        scr.off = SCR_BASE
        ch_h1 = emit_ln_transpose(x_in, hT, gate=[(cst, CST), (ld, LDF)])
        H1EV = ch_h1.n


        if upto < 2:
            P.replay(blk)
            return nc, P
        wslab = cv.take(2 * c.DCH * 512, BF16)
        stg = cv.take(2 * c.TT * 512, BF16)
        ch_wl = [P.chan(f"wl{i}") for i in range(2)]
        ch_wf = [P.chan(f"wf{i}") for i in range(2)]
        ch_ps = [P.chan(f"ps{i}") for i in range(4)]
        ch_pf = [P.chan(f"pf{i}") for i in range(4)]
        ch_sg = [P.chan(f"sg{i}") for i in range(2)]
        ch_sdq = P.chan("sdq")
        ch_sdk = P.chan("sdk")
        ch_sdv = P.chan("sdv")
        stage_last = [None, None]

        NEB = 3 * c.EB1
        cc_k = P.chan("cc_k")
        cc_q = P.chan("cc_q")
        cc_v = P.chan("cc_v")
        wsrc = wqkvT.rearrange("(ch p) e -> p ch e", p=128)
        ev_per_tile = c.DCH // 8

        def wslab_half(slot):
            return wslab[:, slot * c.DCH * 512:(slot + 1) * c.DCH * 512]

        def load_slab(src_view, col0, target, wait, sigch):
            segs = max(1, c.DCH // 8)
            per = c.DCH // segs
            for sgi in range(segs):
                P.step("gpsimd", lambda e, s=sgi, sv=src_view, t=target, c0=col0, p=per: e.dma_start(
                    out=t.rearrange("p (ch k) -> p ch k", k=512)[:, s * p:(s + 1) * p, :],
                    in_=sv[:, s * p:(s + 1) * p, c0:c0 + 512]),
                    wait=wait if sgi == 0 else (), sig=[(sigch, 16)], isdma=True)
            return 16 * segs

        eb_order = (list(range(c.EB1, 2 * c.EB1)) +
                    list(range(2 * c.EB1, 3 * c.EB1)) +
                    list(range(0, c.EB1)))
        for idx, eb in enumerate(eb_order):
            sl = idx % 2
            w = [ch_wf[sl]] if idx >= 2 else []
            load_slab(wsrc, eb * 512, wslab_half(sl), w, ch_wl[sl])
            for tt in range(c.TT):
                pslot = tt % 4
                for ci in range(c.DCH):
                    wl = []
                    if ci == 0:
                        wl = [ch_wl[sl], (ch_h1, (tt + 1) * ev_per_tile), ch_pf[pslot]]
                    sg = []
                    if ci == c.DCH - 1:
                        sg = [(ch_ps[pslot], 1)]
                        if tt == c.TT - 1:
                            sg.append((ch_wf[sl], 1))
                    P.step("tensor", lambda e, p2=pslot, s=sl, c2=ci, t2=tt: e.matmul(
                        ps_mm[p2],
                        hT.rearrange("p (ch t) -> p ch t", t=c.TOK)[:, c2,
                                     t2 * 128:(t2 + 1) * 128],
                        wslab_half(s).rearrange("p (ch k) -> p ch k", k=512)[:, c2, :],
                        start=(c2 == 0), stop=(c2 == c.DCH - 1)),
                        wait=wl, sig=sg)
                P.step("vector", lambda e, p2=pslot, s=sl, t2=tt: e.tensor_scalar(
                    stg[:, (s * c.TT + t2) * 512:(s * c.TT + t2 + 1) * 512],
                    ps_mm[p2], -c.CLIP, c.CLIP, op0=ALU.max, op1=ALU.min),
                    wait=[ch_ps[pslot]] + ([stage_last[sl]] if (tt == 0 and stage_last[sl] is not None) else []),
                    sig=[(ch_pf[pslot], 1), (ch_sg[sl], 1)])
            sec = eb // c.EB1
            ebr = eb % c.EB1
            stg_v = stg[:, sl * c.TT * 512:(sl + 1) * c.TT * 512].rearrange(
                "p (t k) -> p t k", k=512)
            if sec == 0:
                dst = qstage.rearrange("(t p) e -> p t e", p=128)[:, :, ebr * 512:(ebr + 1) * 512]
            elif sec == 1:
                dst = kstage.rearrange("(t p) e -> p t e", p=128)[:, :, ebr * 512:(ebr + 1) * 512]
            else:
                dst = agv_in.rearrange("(t p) e -> p t e", p=128)[:, :, ebr * 512:(ebr + 1) * 512]
            ch_sec = ch_sdq if sec == 0 else (ch_sdk if sec == 1 else ch_sdv)
            P.step("sync", lambda e, d=dst, sv=stg_v: e.dma_start(out=d, in_=sv),
                   wait=[ch_sg[sl]], sig=[(ch_sec, 16)])
            stage_last[sl] = ch_sec
            if idx == c.EB1 - 1:
                scr.off = SCR_BASE
                ch_k_sd = emit_qk_ln(kstage, klnw, klnb, 0, [ch_sdk])
                P.step("gpsimd", lambda e: e.collective_compute(
                    "AllGather", ALU.bypass, replica_groups=RG,
                    ins=[agkq_in[0:c.D, :]], outs=[agk_out[:, :]]),
                    wait=[ch_k_sd], sig=[(cc_k, 1)], isdma=True)
            if idx == 2 * c.EB1 - 1:
                P.step("gpsimd", lambda e: e.collective_compute(
                    "AllGather", ALU.bypass, replica_groups=RG,
                    ins=[agv_in[:, :]], outs=[agv_out[:, :]]),
                    wait=[ch_sdv], sig=[(cc_v, 1)], isdma=True)


        if upto < 3:
            P.replay(blk)
            return nc, P
        ch_q_sd = emit_qk_ln(qstage, qlnw, qlnb, c.D, [ch_sdq])
        P.step("gpsimd", lambda e: e.collective_compute(
            "AllGather", ALU.bypass, replica_groups=RG,
            ins=[agkq_in[c.D:2 * c.D, :]], outs=[agq_out[:, :]]),
            wait=[ch_q_sd], sig=[(cc_q, 1)], isdma=True)


        if upto < 5:
            P.replay(blk)
            return nc, P
        cv.pop()
        cv.push()
        VS = (c.S // 128) * 256
        pairKQ = cv.take(2 * 4 * c.S, BF16)
        pairV = cv.take(2 * VS, BF16)
        sbuf_Sb2 = [cv.take(c.S, F32), cv.take(c.S, F32)]
        pbuf = cv.take(c.KCH, BF16)
        ptbuf = cv.take(2 * c.KCH, BF16)
        astage2 = [cv.take(128, BF16), cv.take(128, BF16)]
        rlrow2 = [rlrow[0:1, 0:128], cv.take(128, F32)[0:1, 0:128]]
        rlb2 = [rlb, cv.take(128, F32)]

        lcol2 = [tcol0[:, 0:1], tcol0[:, 8:9]]
        mrun2 = [tcol0[:, 1:2], tcol0[:, 9:10]]
        bmcol2 = [tcol0[:, 2:3], tcol0[:, 10:11]]
        bm22 = [tcol0[:, 3:4], tcol0[:, 11:12]]
        tmpc2 = [tcol0[:, 4:5], tcol0[:, 12:13]]
        bcol2 = [tcol0[:, 5:6], tcol0[:, 13:14]]
        rlc2 = [tcol0[:, 6:7], tcol0[:, 14:15]]
        acc12 = [tcol1[:, 0:1], tcol1[:, 1:2]]

        ch_pl = [P.chan(f"p5_pl{i}") for i in range(2)]
        ch_pfree = [P.chan(f"p5_pfree{i}") for i in range(2)]
        ch_pes = [P.chan(f"p5_pes{i}") for i in range(2)]
        ch_dsb = [P.chan(f"p5_dsb{i}") for i in range(2)]
        ch_gmx = P.chan("p5_gmx")
        ch_msk = P.chan("p5_msk")
        ch_bm = P.chan("p5_bm")
        ch_exp = P.chan("p5_exp")
        ch_lad = P.chan("p5_lad")
        ch_ptp = P.chan("p5_ptp")
        ch_pte = P.chan("p5_pte")
        ch_pvg = P.chan("p5_pvg")
        ch_oev = P.chan("p5_oev")
        ch_pvd = P.chan("p5_pvd")
        ch_fin = P.chan("p5_fin")
        ch_adr = P.chan("p5_adr")

        gk = agk_out.rearrange("(i rb hl p) t -> i rb hl p t", rb=c.G, hl=c.HL, p=128)
        gq = agq_out.rearrange("(i rb hl p) t -> i rb hl p t", rb=c.G, hl=c.HL, p=128)
        gv = agv_out.rearrange("(kt p) (rb gp q) -> p kt rb gp q", p=128, rb=c.G, q=256)
        pp_glob = [0]
        exp_qt_end = [0, 0]
        adr_at = [0, 0]

        def kq_col(slot, hh, kq, k):
            return (slot * 4 * c.S + hh * 2 * c.S +
                    (k // c.TOK) * 2 * c.TOK + kq * c.TOK + (k % c.TOK))

        def kq_sl(slot, hh, kq, k, w):
            b = kq_col(slot, hh, kq, k)
            if (k % c.TOK) + w <= c.TOK:
                return pairKQ[:, b:b + w]
            assert k % c.TOK == 0 and w % c.TOK == 0
            nb = w // c.TOK
            return pairKQ[:, b:b + (nb - 1) * 2 * c.TOK + c.TOK].rearrange(
                "p (b2 t) -> p b2 t", t=c.TOK)[:, ::2, :] if False else bass.AP(
                tensor=pairKQ.tensor, offset=pairKQ.offset + b,
                ap=[list(pairKQ.ap[0]), [2 * c.TOK, nb], [1, c.TOK]])
        ptlast = [None, None]

        pend5 = [None]

        def emit_passA(ctx):
            sl, hh, qt, qp = ctx["sl"], ctx["hh"], ctx["qt"], ctx["qp"]
            scol = ctx["scol"]
            Sb = sbuf_Sb2[qp]
            bmcol, bm2v = bmcol2[qp], bm22[qp]
            tmpc, mrun = tmpc2[qp], mrun2[qp]
            klen = 128 * (qt + 1)
            nch = (klen + c.KCH - 1) // c.KCH
            ctx["nch"] = nch
            ctx["nkt"] = klen // 128
            ctx["klen"] = klen
            for chi in range(nch):
                k0 = chi * c.KCH
                w = min(c.KCH, klen - k0)
                pp = pp_glob[0] % 2
                pp_glob[0] += 1
                ctx.setdefault("pps", []).append(pp)
                pss = ps_s0 if pp == 0 else ps_s1
                nmm = (w + 511) // 512
                for j in range(nmm):
                    wj = min(512, w - j * 512)
                    wl = []
                    if j == 0:
                        wl = [ch_dsb[pp]]
                        if ctx["first"]:
                            wl.append(ch_pl[sl])
                    qsl = kq_sl(sl, hh, 1, qt * 128, 128)
                    ksl = kq_sl(sl, hh, 0, k0 + j * 512, wj)
                    P.step("tensor", lambda e, j2=j, w2=wj,
                           ps=pss, kh=ksl, qh=qsl: e.matmul(
                        ps[:, j2 * 512:j2 * 512 + w2],
                        qh, kh, start=True, stop=True),
                        wait=wl,
                        sig=[(ch_pes[pp], 1)] if j == nmm - 1 else ())
                P.step("vector", lambda e, w2=w, k2=k0, ps=pss, sc=scol, sb=Sb:
                       e.scalar_tensor_tensor(
                    sb[:, k2:k2 + w2], r0n[:, 127:127 + w2], sc,
                    ps[:, 0:w2], op0=ALU.mult, op1=ALU.add),
                    wait=[ch_pes[pp], (ch_exp, exp_qt_end[qp])],
                    sig=[(ch_dsb[pp], 1)])
                last = (chi == nch - 1)
                if last:
                    P.step("gpsimd", lambda e, kl=klen, sb=Sb: e.affine_select(
                        sb[:, kl - 128:kl], sb[:, kl - 128:kl],
                        [[-1, 128]], ALU.is_ge, fillreg(e, NEG_BIG),
                        base=0, channel_multiplier=1),
                        wait=[ch_dsb[pp]], sig=[(ch_msk, 1)])
                P.step("vector", lambda e, w2=w, k2=k0, sb=Sb, bc=bmcol: e.reduce_max(
                    bc, sb[:, k2:k2 + w2], axis=AX.X),
                    wait=[ch_dsb[pp]] + ([ch_msk] if last else []),
                    sig=[(ch_bm, 1)])
                P.step("vector", lambda e, v=float(k0 - 128 * qt), sc=scol, t3=tmpc:
                       e.tensor_scalar_mul(t3, sc, v), wait=[ch_bm])
                P.step("vector", lambda e, b2=bm2v, bc=bmcol, t3=tmpc: e.tensor_add(b2, bc, t3))
                if chi == 0:
                    P.step("vector", lambda e, m=mrun, b2=bm2v: e.tensor_copy(m, b2),
                           sig=[(ch_gmx, 1)])
                else:
                    P.step("vector", lambda e, m=mrun, b2=bm2v: e.tensor_max(m, m, b2),
                           sig=[(ch_gmx, 1)])
            ctx["gmx_end"] = ch_gmx.n

        def emit_passB(ctx):
            sl, hh, qt, qp = ctx["sl"], ctx["hh"], ctx["qt"], ctx["qp"]
            hloc, scol = ctx["hloc"], ctx["scol"]
            nch, nkt, klen = ctx["nch"], ctx["nkt"], ctx["klen"]
            Sb = sbuf_Sb2[qp]
            lcol, mrun = lcol2[qp], mrun2[qp]
            tmpc, bcol = tmpc2[qp], bcol2[qp]
            rlc, acc1 = rlc2[qp], acc12[qp]
            ktg = 0
            for chi in range(nch):
                k0 = chi * c.KCH
                w = min(c.KCH, klen - k0)
                nkt_ch = w // 128
                lastch = (chi == nch - 1)
                P.step("vector", lambda e, v=float(k0 - 128 * qt), sc=scol, t3=tmpc:
                       e.tensor_scalar_mul(t3, sc, v),
                       wait=[(ch_exp, ch_exp.n), (ch_gmx, ctx["gmx_end"])])
                P.step("vector", lambda e, b=bcol, t3=tmpc, m=mrun: e.tensor_sub(b, t3, m),
                       sig=[(ch_gmx, 1)])
                P.step("scalar", lambda e, w2=w, k2=k0, sb=Sb, b=bcol, a=acc1: e.activation(
                    pbuf[:, 0:w2], sb[:, k2:k2 + w2], AF.Exp,
                    bias=b, accum_out=a),
                    wait=[ch_gmx, (ch_ptp, ch_ptp.n), (ch_lad, ch_lad.n)],
                    sig=[(ch_exp, 1)])
                if chi == 0:
                    P.step("vector", lambda e, l=lcol, a=acc1: e.tensor_copy(l, a),
                           wait=[ch_exp], sig=[(ch_lad, 1)])
                else:
                    P.step("vector", lambda e, l=lcol, a=acc1: e.tensor_add(l, l, a),
                           wait=[ch_exp], sig=[(ch_lad, 1)])
                if lastch:
                    exp_qt_end[qp] = ch_exp.n
                    P.step("vector", lambda e, r=rlc, l=lcol: e.reciprocal(r, l),
                           sig=[(ch_fin, 1)])
                hf = tp_half()
                war = tp_state["last"][hf]
                pslot = chi % 2
                for j in range(nkt_ch):
                    P.step("tensor", lambda e, k2=j, h2=hf: e.transpose(
                        ps_t[:, h2 * 1024 + k2 * 128:h2 * 1024 + (k2 + 1) * 128],
                        pbuf[:, k2 * 128:(k2 + 1) * 128], id_bf[:, :]),
                        wait=([ch_exp] + ([war] if war else [])) if j == 0 else (),
                        sig=[(ch_ptp, 1)] if j == nkt_ch - 1 else ())
                if lastch:
                    P.step("tensor", lambda e, r=rlc: e.transpose(
                        ps_w[0:1, 0:128], r, id_f32[:, :]),
                        wait=[ch_fin], sig=[(ch_fin, 1)])
                    P.step("vector", lambda e, rr=rlrow2[qp]: e.tensor_copy(
                        rr, ps_w[0:1, 0:128]),
                        wait=[ch_fin], sig=[(ch_fin, 1)])
                    P.step("tensor", lambda e, rr=rlrow2[qp]: e.matmul(
                        ps_w[:, 128:256], ones_f32[0:1, :], rr,
                        start=True, stop=True), wait=[ch_fin], sig=[(ch_fin, 1)])
                    P.step("vector", lambda e, rb=rlb2[qp]: e.tensor_copy(
                        rb, ps_w[:, 128:256]),
                        wait=[ch_fin], sig=[(ch_fin, 1)])
                P.step("vector", lambda e, h2=hf, n2=nkt_ch, p2=pslot: e.tensor_copy(
                    ptbuf[:, p2 * c.KCH:p2 * c.KCH + n2 * 128],
                    ps_t[:, h2 * 1024:h2 * 1024 + n2 * 128]),
                    wait=[ch_ptp] + ([ptlast[pslot]] if ptlast[pslot] else []),
                    sig=[(ch_pte, 1)])
                tp_state["last"][hf] = (ch_pte, ch_pte.n)
                for j in range(nkt_ch):
                    kg = ktg + j
                    wl = [ch_pte] if j == 0 else []
                    if kg == 0:
                        wl.append(ch_oev)
                    sg = []
                    if j == nkt_ch - 1:
                        sg.append((ch_pvg, 1))
                    if kg == nkt - 1:
                        sg.append((ch_pvd, 1))
                        if qt == c.QT - 1 and hh == 1:
                            sg.append((ch_pfree[sl], 1))
                    P.step("tensor", lambda e, kg2=kg, j2=j, s=sl, h3=hh, p2=pslot, nk=nkt: e.matmul(
                        ps_o[:, 0:128],
                        pairV[:, s * VS:(s + 1) * VS].rearrange(
                            "p (kt q) -> p kt q", q=256)[:, kg2,
                            h3 * 128:(h3 + 1) * 128],
                        ptbuf[:, p2 * c.KCH + j2 * 128:p2 * c.KCH + (j2 + 1) * 128],
                        start=(kg2 == 0), stop=(kg2 == nk - 1)),
                        wait=wl, sig=sg)
                ptlast[pslot] = (ch_pvg, ch_pvg.n)
                ktg += nkt_ch
            P.step("vector", lambda e, a=astage2[qp], rb=rlb2[qp]: e.tensor_tensor(
                a, ps_o[:, 0:128], rb, ALU.mult),
                wait=[ch_fin, ch_pvd, (ch_adr, ch_adr.n)],
                sig=[(ch_oev, 1)])
            adst = agat_in.rearrange("(j h p) t -> p j h t", p=128, h=c.HL)
            P.step("sync", lambda e, q2=qt, h2=hloc, d=adst, a=astage2[qp]: e.dma_start(
                out=d[:, q2 // c.TT, h2, (q2 % c.TT) * 128:(q2 % c.TT + 1) * 128],
                in_=a),
                wait=[ch_oev], sig=[(ch_adr, 16)])

        npair = c.HL // 2
        for pr in range(npair):
            sl = pr % 2
            wl0 = [cc_k, cc_q, cc_v] if pr == 0 else [ch_pfree[sl]]
            for hh in range(2):
                for kq, gsrc in ((0, gk), (1, gq)):
                    o3 = pairKQ[:, sl * 4 * c.S + hh * 2 * c.S:
                                sl * 4 * c.S + (hh + 1) * 2 * c.S].rearrange(
                        "p (i kq t) -> p kq i t", kq=2, t=c.TOK)[:, kq]
                    P.step("gpsimd", lambda e, p2=pr, h2=hh, o=o3, g3=gsrc: e.dma_start(
                        out=o,
                        in_=g3[:, rankof(e), p2 * 2 + h2].transpose([1, 0, 2])),
                        wait=wl0 if (hh == 0 and kq == 0) else (),
                        sig=[(ch_pl[sl], 16)], isdma=True)
            P.step("gpsimd", lambda e, s=sl, p2=pr: e.dma_start(
                out=pairV[:, s * VS:(s + 1) * VS].rearrange("p (kt q) -> p kt q", q=256),
                in_=gv[:, :, rankof(e), p2, :]),
                sig=[(ch_pl[sl], 16)], isdma=True)

            for hh in range(2):
                hloc = pr * 2 + hh
                for qt in range(c.QT):
                    cur = dict(sl=sl, hh=hh, hloc=hloc, qt=qt, qp=qt % 2,
                               scol=slope_col[:, hloc:hloc + 1],
                               first=(qt == 0 and hh == 0))
                    emit_passA(cur)
                    if pend5[0] is not None:
                        emit_passB(pend5[0])
                    pend5[0] = cur
            if pend5[0] is not None:
                emit_passB(pend5[0])
                pend5[0] = None

        cc_at = P.chan("cc_at")
        P.step("gpsimd", lambda e: e.collective_compute(
            "AllGather", ALU.bypass, replica_groups=RG,
            ins=[agat_in[:, :]], outs=[agat_out[:, :]]),
            wait=[ch_adr], sig=[(cc_at, 1)], isdma=True)
        cv.pop()


        if upto < 6:
            P.replay(blk)
            return nc, P
        cv.push()
        attnT = cv.take(c.DCH * c.TOK, BF16)
        wslab6 = cv.take(2 * c.DCH * 512, BF16)
        xtl = [cv.take(512, F32), cv.take(512, F32)]
        x1s = [cv.take(512, F32), cv.take(512, F32)]

        ch_atl = P.chan("p6_atl")
        ch_xl = [P.chan(f"xl{i}") for i in range(2)]
        ch_xf = [P.chan(f"xf{i}") for i in range(2)]
        ch_ev = [P.chan(f"ev{i}") for i in range(2)]
        ch_x1sd = P.chan("p6_x1sd")

        gat = agat_out.rearrange("(i j h p) t -> i j h p t", j=c.G, h=c.HL, p=128)
        for i in range(c.G):
            P.step("gpsimd", lambda e, i2=i: e.dma_start(
                out=attnT.rearrange("p (i h t) -> p i h t", i=c.G, h=c.HL)[:, i2],
                in_=gat[i2, rankof(e)].transpose([1, 0, 2])),
                wait=[cc_at] if i == 0 else (), sig=[(ch_atl, 16)], isdma=True)
        ATL = ch_atl.n

        wsrc6 = woutT.rearrange("(ch p) e -> p ch e", p=128)

        def wslab6_half(slot):
            return wslab6[:, slot * c.DCH * 512:(slot + 1) * c.DCH * 512]

        for eb in range(c.EB1):
            sl = eb % 2
            w = [ch_wf[sl]] + ([cc_at] if eb < 2 else [])
            segs = max(1, c.DCH // 8)
            per = c.DCH // segs
            for sgi in range(segs):
                P.step("gpsimd", lambda e, s=sgi, c0=eb * 512, t=wslab6_half(sl), p=per: e.dma_start(
                    out=t.rearrange("p (ch k) -> p ch k", k=512)[:, s * p:(s + 1) * p, :],
                    in_=wsrc6[:, s * p:(s + 1) * p, c0:c0 + 512]),
                    wait=w if sgi == 0 else (), sig=[(ch_wl[sl], 16)], isdma=True)
            for tt in range(c.TT):
                pslot = tt % 4
                xsl = tt % 2
                dma("sync", xtl[xsl], x_in[tt * 128:(tt + 1) * 128, eb * 512:(eb + 1) * 512],
                    wait=([ch_xf[xsl]] if ch_xf[xsl].n > 0 else []) +
                         ([cc_at] if eb == 0 and tt < 2 else []),
                    sig=[(ch_xl[xsl], 16)])
                for ci in range(c.DCH):
                    wl = []
                    if ci == 0:
                        wl = [ch_wl[sl], (ch_atl, ATL), ch_pf[pslot]]
                    sg = []
                    if ci == c.DCH - 1:
                        sg = [(ch_ps[pslot], 1)]
                        if tt == c.TT - 1:
                            sg.append((ch_wf[sl], 1))
                    P.step("tensor", lambda e, p2=pslot, s=sl, c2=ci, t2=tt: e.matmul(
                        ps_mm[p2],
                        attnT.rearrange("p (ch t) -> p ch t", t=c.TOK)[:, c2,
                                        t2 * 128:(t2 + 1) * 128],
                        wslab6_half(s).rearrange("p (ch k) -> p ch k", k=512)[:, c2, :],
                        start=(c2 == 0), stop=(c2 == c.DCH - 1)),
                        wait=wl, sig=sg)
                P.step("vector", lambda e, p2=pslot, xs=xsl: e.tensor_tensor(
                    x1s[xs], ps_mm[p2], xtl[xs], ALU.add),
                    wait=[ch_ps[pslot], ch_xl[xsl], (ch_x1sd, ch_x1sd.n)],
                    sig=[(ch_pf[pslot], 1), (ch_xf[xsl], 1), (ch_ev[xsl], 1)])
                P.step("sync", lambda e, t2=tt, e2=eb, xs=xsl: e.dma_start(
                    out=x1d[t2 * 128:(t2 + 1) * 128, e2 * 512:(e2 + 1) * 512],
                    in_=x1s[xs]),
                    wait=[ch_ev[xsl]], sig=[(ch_x1sd, 16)])
        X1SD = ch_x1sd.n
        cv.pop()


        if upto < 7:
            P.replay(blk)
            return nc, P
        cv.push()
        h2T = cv.take(c.DCH * c.TOK, BF16)
        scr.off = SCR_BASE
        ch_h2 = emit_ln_transpose(x1d, h2T, gate=[(ch_x1sd, X1SD)])
        H2EV = ch_h2.n
        LTP = P.chan("lt_tp").n


        if upto < 8:
            P.replay(blk)
            return nc, P
        dslab8 = cv.take(2 * c.RC * 512, BF16)
        bslab8 = cv.take(2 * c.DCH * 512, BF16)
        scr.off = SCR_BASE
        basebrow = scr.take(c.D, BF16)
        atslab = scr.take(c.DCH * 64, BF16)
        rwslab = scr.take(c.DCH * 16, BF16)
        resTd = scr.take(c.TOK, F32)
        sstack = scr.take(c.RC * c.TOK, BF16)
        combT = scr.take(128, F32)
        crow = [scr.take(c.TOK, F32), scr.take(c.TOK, F32)]
        lgt = scr.take(16, F32)
        prb = scr.take(16, F32)
        xx2 = scr.take(16, F32)
        eq1 = scr.take(16, F32)
        eq2 = scr.take(16, F32)
        comb = scr.take(16, F32)
        xtl8 = [scr.take(512, F32), scr.take(512, F32)]
        outs8 = [scr.take(512, F32), scr.take(512, F32)]
        rcol = tcol0[:, 8:9]
        m1c = tcol0[:, 9:10]
        m2c = tcol0[:, 10:11]
        denc = tcol0[:, 11:12]
        w1c = tcol0[:, 12:13]
        w2c = tcol0[:, 13:14]
        nm1c = tcol0[:, 14:15]
        sumc = tcol0[:, 15:16]

        ch_cl8 = P.chan("p8_cl")
        ch_lg = P.chan("p8_lg")
        ch_rt = P.chan("p8_rt")
        ch_cmb = P.chan("p8_cmb")
        ch_res = P.chan("p8_res")
        ch_sst = P.chan("p8_sst")
        ch_out = P.chan("out_dma")

        atsrc = at_in.rearrange("(ch p) r -> p ch r", p=128)
        rwsrc = rwt_in.rearrange("(ch p) r -> p ch r", p=128)
        ch_ltp = P.chan("lt_tp")
        P.step("gpsimd", lambda e: e.dma_start(
            out=atslab.rearrange("p (ch r) -> p ch r", r=64), in_=atsrc),
            wait=[(ch_ltp, LTP)], sig=[(ch_cl8, 16)], isdma=True)
        P.step("gpsimd", lambda e: e.dma_start(
            out=rwslab.rearrange("p (ch r) -> p ch r", r=16), in_=rwsrc),
            sig=[(ch_cl8, 16)], isdma=True)
        dma("gpsimd", basebrow[0:1, :], baseb_in, sig=[(ldg, 16)])

        for tt in range(c.TT):
            for ci in range(c.DCH):
                P.step("tensor", lambda e, c2=ci, t2=tt: e.matmul(
                    ps_w[:, 0:16],
                    h2T.rearrange("p (ch t) -> p ch t", t=c.TOK)[:, c2,
                                  t2 * 128:(t2 + 1) * 128],
                    rwslab.rearrange("p (ch r) -> p ch r", r=16)[:, c2, :],
                    start=(c2 == 0), stop=(c2 == c.DCH - 1)),
                    wait=[(ch_cl8, 32), (ch_h2, H2EV), (ch_rt, ch_rt.n)] if ci == 0 else (),
                    sig=[(ch_lg, 1)] if ci == c.DCH - 1 else ())
            P.step("vector", lambda e: e.tensor_copy(lgt, ps_w[:, 0:16]),
                   wait=[ch_lg], sig=[(ch_rt, 1)])
            P.step("vector", lambda e: e.reduce_max(m1c, lgt, axis=AX.X))
            P.step("vector", lambda e: e.tensor_scalar_mul(nm1c, m1c, -1.0),
                   sig=[(ch_rt, 1)])
            P.step("scalar", lambda e: e.activation(
                prb, lgt, AF.Exp, bias=nm1c, accum_out=sumc),
                wait=[ch_rt], sig=[(ch_rt, 1)])
            P.step("vector", lambda e: e.reciprocal(rcol, sumc), wait=[ch_rt])
            P.step("vector", lambda e: e.tensor_scalar(
                prb, prb, rcol, None, op0=ALU.mult))
            P.step("vector", lambda e: e.reduce_max(m1c, prb, axis=AX.X))
            P.step("vector", lambda e: e.tensor_scalar(
                eq1, prb, m1c, None, op0=ALU.is_equal))
            P.step("vector", lambda e: e.tensor_scalar_mul(nm1c, m1c, -1.0))
            P.step("vector", lambda e: e.scalar_tensor_tensor(
                xx2, eq1, nm1c, prb, op0=ALU.mult, op1=ALU.add))
            P.step("vector", lambda e: e.reduce_max(m2c, xx2, axis=AX.X))
            P.step("vector", lambda e: e.tensor_add(denc, m1c, m2c))
            P.step("vector", lambda e: e.tensor_scalar(
                denc, denc, 1.0e-6, None, op0=ALU.add))
            P.step("vector", lambda e: e.reciprocal(denc, denc))
            P.step("vector", lambda e: e.tensor_mul(w1c, m1c, denc))
            P.step("vector", lambda e: e.tensor_mul(w2c, m2c, denc))
            P.step("vector", lambda e: e.tensor_scalar(
                eq2, xx2, m2c, None, op0=ALU.is_equal))
            P.step("vector", lambda e: e.tensor_scalar(
                eq2, eq2, w2c, None, op0=ALU.mult))
            P.step("vector", lambda e: e.scalar_tensor_tensor(
                comb, eq1, w1c, eq2, op0=ALU.mult, op1=ALU.add),
                sig=[(ch_rt, 1)])
            P.step("tensor", lambda e: e.transpose(
                ps_w[0:16, 128:256], comb, id_f32[:, :]),
                wait=[ch_rt, (ch_cmb, ch_cmb.n)], sig=[(ch_rt, 1)])
            P.step("vector", lambda e: e.tensor_copy(
                combT[0:16, 0:128], ps_w[0:16, 128:256]),
                wait=[ch_rt, (ch_cmb, ch_cmb.n)], sig=[(ch_rt, 1), (ch_cmb, 1)])
            P.step("sync", lambda e, t2=tt: e.dma_start(
                out=combT_d[:, t2 * 128:(t2 + 1) * 128], in_=combT[0:16, 0:128]),
                wait=[(ch_cmb, ch_cmb.n)], sig=[(ch_cmb, 16)])
        CMB = ch_cmb.n
        if upto == 81:
            P.replay(blk)
            return nc, P

        NB2 = c.TOK // c.TB
        for tb in range(NB2):
            for half in range(2):
                for ci in range(c.DCH):
                    sg = []
                    if ci == c.DCH - 1 and half == 1:
                        sg = [(ch_res, 1)]
                    P.step("tensor", lambda e, c2=ci, t2=tb, h2=half: e.matmul(
                        ps_o[h2 * 64:(h2 + 1) * 64, 0:c.TB],
                        atslab.rearrange("p (ch r) -> p ch r", r=64)[:, c2, :],
                        h2T.rearrange("p (ch t) -> p ch t", t=c.TOK)[:, c2,
                                      t2 * c.TB:(t2 + 1) * c.TB],
                        start=(c2 == 0), stop=(c2 == c.DCH - 1)),
                        wait=[(ch_cl8, 32), (ch_h2, H2EV), (ch_res, ch_res.n)]
                             if (ci == 0 and half == 0) else (),
                        sig=sg)
            P.step("vector", lambda e, t2=tb: e.tensor_copy(
                resTd[:, t2 * c.TB:(t2 + 1) * c.TB], ps_o[:, 0:c.TB]),
                wait=[ch_res], sig=[(ch_res, 1)])
        RESN = ch_res.n
        if upto == 82:
            P.replay(blk)
            return nc, P


        ch_srow = P.chan("p8_srow")
        crow2 = crow[0]
        for rc in range(c.RC):
            P.step("gpsimd", lambda e, e2=rc * 2: e.dma_start(
                out=crow2[0:2, :], in_=combT_d[e2:e2 + 2, :]),
                wait=[(ch_cmb, CMB), (ch_ltp, LTP)] +
                     ([(ch_sst, ch_sst.n)] if rc > 0 else []),
                sig=[(ch_srow, 16)], isdma=True)
            for tb in range(NB2):
                P.step("tensor", lambda e, t2=tb: e.matmul(
                    ps_w[:, 0:c.TB],
                    selm[0:2, :],
                    crow2[0:2, t2 * c.TB:(t2 + 1) * c.TB],
                    start=True, stop=True),
                    wait=[ch_srow, (ch_sst, ch_sst.n)],
                    sig=[(ch_sst, 1)])
                P.step("vector", lambda e, r2=rc, t2=tb: e.tensor_tensor(
                    sstack[:, r2 * c.TOK + t2 * c.TB:r2 * c.TOK + (t2 + 1) * c.TB],
                    ps_w[:, 0:c.TB], resTd[:, t2 * c.TB:(t2 + 1) * c.TB], ALU.mult),
                    wait=[ch_sst, (ch_res, RESN)], sig=[(ch_sst, 1)])

        SSTN = ch_sst.n
        if upto == 83:
            P.replay(blk)
            return nc, P


        bsrc = basewT.rearrange("(ch p) e -> p ch e", p=128)
        dsrc = bstk_in.rearrange("(rc p) e -> p rc e", p=128)

        def bslab8_half(slot):
            return bslab8[:, slot * c.DCH * 512:(slot + 1) * c.DCH * 512]

        def dslab8_half(slot):
            return dslab8[:, slot * c.RC * 512:(slot + 1) * c.RC * 512]

        for eb in range(c.EB1):
            sl = eb % 2
            w = [ch_wf[sl]] + ([(ch_ltp, LTP)] if eb < 2 else [])
            segs = max(1, c.DCH // 8)
            per = c.DCH // segs
            for sgi in range(segs):
                P.step("gpsimd", lambda e, s=sgi, c0=eb * 512, t=bslab8_half(sl), p=per: e.dma_start(
                    out=t.rearrange("p (ch k) -> p ch k", k=512)[:, s * p:(s + 1) * p, :],
                    in_=bsrc[:, s * p:(s + 1) * p, c0:c0 + 512]),
                    wait=w if sgi == 0 else (), sig=[(ch_wl[sl], 16)], isdma=True)
            P.step("gpsimd", lambda e, c0=eb * 512, t=dslab8_half(sl): e.dma_start(
                out=t.rearrange("p (rc k) -> p rc k", k=512),
                in_=dsrc[:, :, c0:c0 + 512]),
                sig=[(ch_wl[sl], 16)], isdma=True)
            for tt in range(c.TT):
                pslot = tt % 4
                xsl = tt % 2
                dma("sync", xtl8[xsl], x1d[tt * 128:(tt + 1) * 128, eb * 512:(eb + 1) * 512],
                    wait=[(ch_x1sd, X1SD), (ch_ltp, LTP)] +
                         ([ch_xf[xsl]] if ch_xf[xsl].n > 0 else []),
                    sig=[(ch_xl[xsl], 16)])
                skip_delta = False
                skip_bias = False
                for ci in range(c.DCH):
                    wl = []
                    if ci == 0:
                        wl = [ch_wl[sl], (ch_sst, SSTN), ch_pf[pslot], (ldg, 16)]
                    P.step("tensor", lambda e, p2=pslot, s=sl, c2=ci, t2=tt: e.matmul(
                        ps_mm[p2],
                        h2T.rearrange("p (ch t) -> p ch t", t=c.TOK)[:, c2,
                                      t2 * 128:(t2 + 1) * 128],
                        bslab8_half(s).rearrange("p (ch k) -> p ch k", k=512)[:, c2, :],
                        start=(c2 == 0), stop=(skip_delta and skip_bias and c2 == c.DCH - 1)),
                        wait=wl)
                if not skip_bias:
                    P.step("tensor", lambda e, p2=pslot, e2=eb: e.matmul(
                        ps_mm[p2], onesrow_bf[0:1, :],
                        basebrow[0:1, e2 * 512:(e2 + 1) * 512],
                        start=False, stop=skip_delta))
                sg = [(ch_ps[pslot], 1)]
                if tt == c.TT - 1:
                    sg.append((ch_wf[sl], 1))
                if not skip_delta:
                    for rc in range(c.RC):
                        P.step("tensor", lambda e, p2=pslot, s=sl, r2=rc, t2=tt: e.matmul(
                            ps_mm[p2],
                            sstack.rearrange("p (rc t) -> p rc t", t=c.TOK)[:, r2,
                                             t2 * 128:(t2 + 1) * 128],
                            dslab8_half(s).rearrange("p (rc k) -> p rc k", k=512)[:, r2, :],
                            start=False, stop=(r2 == c.RC - 1)),
                            sig=sg if rc == c.RC - 1 else ())
                else:
                    P.step("tensor", lambda e: e.drain(), sig=sg)
                P.step("vector", lambda e, p2=pslot, xs=xsl: e.tensor_tensor(
                    outs8[xs], ps_mm[p2], xtl8[xs], ALU.add),
                    wait=[ch_ps[pslot], ch_xl[xsl], (ch_out, ch_out.n)],
                    sig=[(ch_pf[pslot], 1), (ch_xf[xsl], 1), (ch_ev[xsl], 1)])
                P.step("sync", lambda e, t2=tt, e2=eb, xs=xsl: e.dma_start(
                    out=y_out[t2 * 128:(t2 + 1) * 128, e2 * 512:(e2 + 1) * 512],
                    in_=outs8[xs]),
                    wait=[ch_ev[xsl]], sig=[(ch_out, 16)])

        P.step("sync", None, wait=[(ch_out, ch_out.n)])

        P.replay(blk)

    return nc, P



def _shards(cfg, inputs):
    c = cfg
    x = inputs["L0_attn_hidden_states"]
    wqkvT = np.ascontiguousarray(np.asarray(inputs["L0_attn_w_qkv"], np.float32).T)
    woutT = np.ascontiguousarray(np.asarray(inputs["L0_attn_w_out"], np.float32).T)
    basewT = np.ascontiguousarray(np.asarray(inputs["L0_ffn_base_w"], np.float32).T)
    AT = np.ascontiguousarray(np.asarray(inputs["L0_ffn_A"], np.float32).T)
    RWT = np.ascontiguousarray(np.asarray(inputs["L0_ffn_router_w"], np.float32).T)
    Bstk = np.ascontiguousarray(
        np.asarray(inputs["L0_ffn_B"], np.float32).transpose(0, 2, 1).reshape(c.E * c.R, c.D))
    baseb = np.ascontiguousarray(np.asarray(inputs["L0_ffn_base_b"], np.float32).reshape(1, c.D))
    qlnw = np.ascontiguousarray(np.asarray(inputs["L0_attn_q_ln_w"], np.float32).reshape(c.DCH, 128).T)
    qlnb = np.ascontiguousarray(np.asarray(inputs["L0_attn_q_ln_b"], np.float32).reshape(c.DCH, 128).T)
    klnw = np.ascontiguousarray(np.asarray(inputs["L0_attn_k_ln_w"], np.float32).reshape(c.DCH, 128).T)
    klnb = np.ascontiguousarray(np.asarray(inputs["L0_attn_k_ln_b"], np.float32).reshape(c.DCH, 128).T)
    slopes = np.asarray(alibi_slopes(c.H), np.float32)

    in_maps = []
    for core in range(c.NC):
        g, r = core // c.G, core % c.G
        xs = np.ascontiguousarray(np.asarray(x[g, r * c.TOK:(r + 1) * c.TOK, :], np.float32))
        sl = np.ascontiguousarray(slopes[c.HL * r:c.HL * (r + 1)].reshape(1, c.HL))
        in_maps.append({
            "x": xs, "wqkvT": wqkvT, "woutT": woutT, "basewT": basewT,
            "AT": AT, "RWT": RWT, "Bstk": Bstk, "baseb": baseb,
            "qlnw": qlnw, "qlnb": qlnb, "klnw": klnw, "klnb": klnb,
            "slopes": sl,
        })
    return in_maps


def kernel(**inputs):
    cfg = Cfg()
    nc, _ = build(cfg)
    in_maps = _shards(cfg, inputs)
    res = run_bass_kernel_spmd(nc, in_maps, core_ids=list(range(cfg.NC)))
    y = np.zeros((cfg.B, cfg.S, cfg.D), np.float32)
    for core in range(cfg.NC):
        g, r = core // cfg.G, core % cfg.G
        y[g, r * cfg.TOK:(r + 1) * cfg.TOK, :] = np.asarray(res.results[core]["out"])
    return y


if __name__ == "__main__":
    tc = Cfg(D=1024, S=512, H=8, R=64, E=16)
    nc, P = build(tc)
    print("instruction counts:", P.counts())


# version 10
# speedup vs baseline: 1.4872x; 1.4872x over previous
import math

import numpy as np

import concourse.bass as bass
import concourse.mybir as mybir
from concourse.bass_utils import run_bass_kernel_spmd

F32 = mybir.dt.float32
BF16 = mybir.dt.bfloat16
AF = mybir.ActivationFunctionType
ALU = mybir.AluOpType
AX = mybir.AxisListType

NEG_BIG = -1.0e30


class Cfg:
    def __init__(self, D=4096, S=4096, H=32, R=64, E=16):
        self.D = D
        self.S = S
        self.B = 2
        self.H = H
        self.HD = 128
        self.NC = 8
        self.G = 4
        self.R = R
        self.E = E
        assert D == H * 128
        self.TOK = S // self.G
        self.TT = self.TOK // 128
        self.DCH = D // 128
        self.QT = S // 128
        self.HL = H // self.G
        self.EB1 = D // 512
        self.KCH = 1024
        self.RC = (E * R) // 128
        self.CLIP = 8.0
        self.EPS = 1.0e-5
        self.SCALING = 32.0 / 64.0
        self.ISQ = 1.0 / math.sqrt(128.0)
        self.R0W = 128 + min(self.KCH, S)
        assert self.HL % 2 == 0 and self.DCH % 8 == 0
        self.TB = min(512, self.TOK)
        assert self.TOK % 128 == 0 and self.R == 64


def alibi_slopes(h, bias_max=8):
    p2 = 2 ** math.ceil(math.log2(h))
    m = [(i + 1) * (bias_max / p2) for i in range(p2)]
    s = [1.0 / (2.0 ** mi) for mi in m]
    if p2 != h:
        s = (s[1::2] + s[::2])[:h]
    return s


class Chan:
    def __init__(self, nc, name):
        self.h = nc.alloc_semaphore(name=name)
        self.n = 0


class Prog:
    ENGINES = ("tensor", "vector", "scalar", "gpsimd", "sync")

    def __init__(self, nc):
        self.nc = nc
        self.streams = {e: [] for e in self.ENGINES}
        self._chans = {}

    def chan(self, name):
        if name not in self._chans:
            self._chans[name] = Chan(self.nc, name)
        return self._chans[name]

    MAX_SIG = 1
    CHAINED = ("vector", "scalar", "gpsimd")

    def step(self, eng, fn, wait=(), sig=(), isdma=False, dep=True):
        waits = []
        for w in wait:
            ch, v = (w if isinstance(w, tuple) else (w, None))
            v = ch.n if v is None else v
            if v > 0:
                waits.append((ch.h, v))
        sigs = []
        for s in sig:
            ch, inc = (s if isinstance(s, tuple) else (s, 1))
            ch.n += inc
            sigs.append((ch.h, inc))
        drain = (eng in self.CHAINED) and dep and not isdma
        self.streams[eng].append((fn, waits, sigs[:self.MAX_SIG], drain))
        for extra in sigs[self.MAX_SIG:]:
            self.streams[eng].append(("DNOP", [], [extra], False))

    def replay(self, blk):
        def mk(eng):
            def body(e):
                first = True
                for fn, waits, sigs, drain in self.streams[eng]:
                    if fn == "DNOP":
                        e.drain()
                        inst = e.nop()
                        for sem, inc in sigs:
                            inst.then_inc(sem, inc)
                        continue
                    if drain and not first:
                        e.drain()
                    first = False
                    for sem, v in waits:
                        e.wait_ge(sem, v)
                    inst = fn(e) if fn is not None else None
                    for sem, inc in sigs:
                        inst.then_inc(sem, inc)
            return body
        blk.tensor(mk("tensor"))
        blk.vector(mk("vector"))
        blk.scalar(mk("scalar"))
        blk.gpsimd(mk("gpsimd"))
        blk.sync(mk("sync"))

    def counts(self):
        return {e: len(s) for e, s in self.streams.items()}


class Carver:
    def __init__(self, arena_ap, total_bytes):
        self.a = arena_ap
        self.total = total_bytes
        self.off = 0
        self.marks = []
        self.peak = 0

    def take(self, n_elems, dtype):
        esz = 4 if dtype == F32 else 2
        al = (self.off + 63) // 64 * 64
        start = al // 2
        nb = n_elems * esz
        self.off = al + nb
        self.peak = max(self.peak, self.off)
        assert self.off <= self.total, f"SBUF arena overflow: {self.off} > {self.total}"
        sl = self.a[:, start:start + nb // 2]
        if dtype == F32:
            sl = sl.bitcast(F32)
        return sl

    def push(self):
        self.marks.append(self.off)

    def pop(self):
        self.off = self.marks.pop()


def build(cfg: Cfg, upto=99):
    c = cfg
    nc = bass.Bass("TRN2", target_bir_lowering=False, debug=False)

    x_in = nc.dram_tensor("x", [c.TOK, c.D], F32, kind="ExternalInput").ap()
    wqkvT = nc.dram_tensor("wqkvT", [c.D, 3 * c.D], F32, kind="ExternalInput").ap()
    woutT = nc.dram_tensor("woutT", [c.D, c.D], F32, kind="ExternalInput").ap()
    basewT = nc.dram_tensor("basewT", [c.D, c.D], F32, kind="ExternalInput").ap()
    at_in = nc.dram_tensor("AT", [c.D, c.R], F32, kind="ExternalInput").ap()
    rwt_in = nc.dram_tensor("RWT", [c.D, c.E], F32, kind="ExternalInput").ap()
    bstk_in = nc.dram_tensor("Bstk", [c.E * c.R, c.D], F32, kind="ExternalInput").ap()
    baseb_in = nc.dram_tensor("baseb", [1, c.D], F32, kind="ExternalInput").ap()
    qlnw_in = nc.dram_tensor("qlnw", [128, c.DCH], F32, kind="ExternalInput").ap()
    qlnb_in = nc.dram_tensor("qlnb", [128, c.DCH], F32, kind="ExternalInput").ap()
    klnw_in = nc.dram_tensor("klnw", [128, c.DCH], F32, kind="ExternalInput").ap()
    klnb_in = nc.dram_tensor("klnb", [128, c.DCH], F32, kind="ExternalInput").ap()
    slop_in = nc.dram_tensor("slopes", [1, c.HL], F32, kind="ExternalInput").ap()

    y_out = nc.dram_tensor("out", [c.TOK, c.D], F32, kind="ExternalOutput").ap()

    qstage = nc.dram_tensor("qstage", [c.TOK, c.D], BF16, kind="Internal").ap()
    kstage = nc.dram_tensor("kstage", [c.TOK, c.D], BF16, kind="Internal").ap()
    x1d = nc.dram_tensor("x1d", [c.TOK, c.D], F32, kind="Internal").ap()
    combT_d = nc.dram_tensor("combT_d", [c.E, c.TOK], F32, kind="Internal").ap()
    agkq_in = nc.dram_tensor("agkq_in", [2 * c.D, c.TOK], BF16, kind="Internal").ap()
    agk_out = nc.dram_tensor("agk_out", [c.G * c.D, c.TOK], BF16, kind="Internal").ap()
    agq_out = nc.dram_tensor("agq_out", [c.G * c.D, c.TOK], BF16, kind="Internal").ap()
    agv_in = nc.dram_tensor("agv_in", [c.TOK, c.D], BF16, kind="Internal").ap()
    agv_out = nc.dram_tensor("agv_out", [c.S, c.D], BF16, kind="Internal").ap()
    agat_in = nc.dram_tensor("agat_in", [c.D, c.TOK], BF16, kind="Internal").ap()
    agat_out = nc.dram_tensor("agat_out", [c.G * c.D, c.TOK], BF16, kind="Internal").ap()

    RG = [[0, 1, 2, 3], [4, 5, 6, 7]]

    with (
        nc.sbuf_tensor("arena", [128, 105472], BF16) as arena_t,
        nc.psum_tensor("ps_s0", [128, 1024], F32) as ps_s0_t,
        nc.psum_tensor("ps_s1", [128, 1024], F32) as ps_s1_t,
        nc.psum_tensor("ps_o", [128, 512], F32) as ps_o_t,
        nc.psum_tensor("ps_t", [128, 2048], BF16) as ps_t_t,
        nc.psum_tensor("ps_w", [128, 512], F32) as ps_w_t,
        nc.Block() as blk,
    ):
        arena = arena_t.ap()
        ps_s0 = ps_s0_t.ap()
        ps_s1 = ps_s1_t.ap()
        ps_o = ps_o_t.ap()
        ps_t = ps_t_t.ap()
        ps_w = ps_w_t.ap()
        ps_mm = [ps_s0[:, 0:512], ps_s0[:, 512:1024],
                 ps_s1[:, 0:512], ps_s1[:, 512:1024]]

        P = Prog(nc)
        cv = Carver(arena, 156672)
        SCR_BASE = 156672
        scr = Carver(arena, 210944)
        scr.off = SCR_BASE

        id_bf = cv.take(128, BF16)
        id_f32 = cv.take(128, F32)
        ones_f32 = cv.take(128, F32)
        sclrow = cv.take(64, F32)
        onesrow_bf = cv.take(128, BF16)
        r0n = cv.take(c.R0W, F32)
        qlnw = cv.take(c.DCH, F32)
        qlnb = cv.take(c.DCH, F32)
        klnw = cv.take(c.DCH, F32)
        klnb = cv.take(c.DCH, F32)
        slope_col = cv.take(max(c.HL, 8), F32)
        sloprow = cv.take(max(c.HL, 8), F32)
        statw = cv.take(6 * (c.D // 512), F32)
        stat2 = cv.take(8, F32)
        tcol0 = cv.take(16, F32)
        tcol1 = cv.take(8, F32)
        rlrow = cv.take(128, F32)
        rlb = cv.take(128, F32)
        epscol = cv.take(1, F32)
        nhalf = cv.take(1, F32)
        selm = cv.take(128, F32)

        mean_c = stat2[:, 0:1]
        var_c = stat2[:, 1:2]
        rstd_c = stat2[:, 2:3]
        negm_c = stat2[:, 3:4]

        _rank_cache = {}
        _fill_cache = {}

        def fillreg(e, v):
            import struct
            k = (id(e), v)
            if k not in _fill_cache:
                _fill_cache[k] = e.to_reg(
                    struct.unpack('<i', struct.pack('<f', v))[0])
            return _fill_cache[k]

        def rankof(e):
            k = id(e)
            if k not in _rank_cache:
                _rank_cache[k] = e.partition_id() % c.G
            return _rank_cache[k]

        def dma(eng, out, in_, wait=(), sig=()):
            P.step(eng, lambda e, o=out, i=in_: e.dma_start(out=o, in_=i),
                   wait=wait, sig=sig, isdma=True)

        ld = P.chan("ld")
        cst = P.chan("cst")
        dma("sync", qlnw, qlnw_in, sig=[(ld, 16)])
        dma("sync", qlnb, qlnb_in, sig=[(ld, 16)])
        dma("sync", klnw, klnw_in, sig=[(ld, 16)])
        dma("sync", klnb, klnb_in, sig=[(ld, 16)])
        dma("sync", sloprow[0:1, 0:c.HL], slop_in, sig=[(ld, 16)])
        ldg = P.chan("ldg")

        P.step("gpsimd", lambda e: e.memset(id_bf[:, :], 1.0))
        P.step("gpsimd", lambda e: e.affine_select(
            id_bf[:, :], id_bf[:, :], [[1, 128]], ALU.is_equal, 0.0,
            base=0, channel_multiplier=-1), sig=[(cst, 1)])
        P.step("gpsimd", lambda e: e.memset(id_f32[:, :], 1.0))
        P.step("gpsimd", lambda e: e.affine_select(
            id_f32[:, :], id_f32[:, :], [[1, 128]], ALU.is_equal, 0.0,
            base=0, channel_multiplier=-1), sig=[(cst, 1)])
        P.step("gpsimd", lambda e: e.memset(ones_f32[0:1, :], 1.0), sig=[(cst, 1)])
        P.step("gpsimd", lambda e: e.memset(sclrow[0:1, :], c.SCALING), sig=[(cst, 1)])
        P.step("gpsimd", lambda e: e.memset(onesrow_bf[0:1, :], 1.0), sig=[(cst, 1)])
        P.step("gpsimd", lambda e: e.memset(epscol, c.EPS), sig=[(cst, 1)])
        P.step("gpsimd", lambda e: e.memset(nhalf, -0.5), sig=[(cst, 1)])
        P.step("gpsimd", lambda e: e.memset(selm[0:2, :], c.SCALING), sig=[(cst, 1)])
        P.step("gpsimd", lambda e: e.affine_select(
            selm[0:2, :], selm[0:2, :], [[1, 128]], ALU.is_ge,
            fillreg(e, 0.0), base=0, channel_multiplier=-64), sig=[(cst, 1)])
        P.step("gpsimd", lambda e: e.affine_select(
            selm[0:2, :], selm[0:2, :], [[-1, 128]], ALU.is_ge,
            fillreg(e, 0.0), base=63, channel_multiplier=64), sig=[(cst, 1)])
        P.step("gpsimd", lambda e: e.iota(
            r0n[:, :], [[1, c.R0W]], base=-127, channel_multiplier=-1,
            allow_small_or_imprecise_dtypes=True),
            sig=[(cst, 1)])
        P.step("vector", lambda e: e.tensor_scalar_mul(qlnw, qlnw, c.ISQ),
               wait=[ld], sig=[(cst, 1)])
        P.step("vector", lambda e: e.tensor_scalar_mul(qlnb, qlnb, c.ISQ),
               wait=[ld], sig=[(cst, 1)])
        P.step("tensor", lambda e: e.matmul(
            ps_w[:, 0:c.HL], ones_f32[0:1, :], sloprow[0:1, 0:c.HL],
            start=True, stop=True), wait=[ld, cst], sig=[(cst, 1)])
        P.step("vector", lambda e: e.tensor_copy(slope_col[:, 0:c.HL], ps_w[:, 0:c.HL]),
               wait=[cst], sig=[(cst, 1)])
        CST = cst.n
        LDF = ld.n

        tp_state = {"par": 0, "last": [None, None]}

        def tp_half():
            h = tp_state["par"] % 2
            tp_state["par"] += 1
            return h

        def emit_ln_transpose(src_dram, hT, gate):
            htile = scr.take(c.D, BF16)
            xt = [scr.take(c.D, F32), scr.take(c.D, F32)]
            ch_x = [P.chan(f"lt_x{i}") for i in range(2)]
            ch_xf = [P.chan(f"lt_xf{i}") for i in range(2)]
            ch_st = P.chan("lt_st")
            ch_h = P.chan("lt_h")
            ch_tp = P.chan("lt_tp")
            ch_ev = P.chan("lt_ev")
            ngrp = c.DCH // 8
            for tt in range(c.TT):
                sl = tt % 2
                src = src_dram[tt * 128:(tt + 1) * 128, :]
                w = list(gate) + ([ch_xf[sl]] if ch_xf[sl].n > 0 else [])
                dma("sync", xt[sl], src, wait=w, sig=[(ch_x[sl], 16)])
                nseg = c.D // 512
                for sgi in range(nseg):
                    P.step("vector", lambda e, s=sgi, xi=xt[sl]: e.bn_stats(
                        statw[:, 6 * s:6 * s + 6], xi[:, 512 * s:512 * (s + 1)]),
                        wait=[ch_x[sl]] if sgi == 0 else (), dep=(sgi == 0))
                P.step("vector", lambda e, n=nseg: e.bn_aggr(
                    stat2[:, 0:2], statw[:, 0:6 * n]))
                P.step("vector", lambda e: e.tensor_scalar_mul(negm_c, mean_c, -1.0),
                       sig=[(ch_st, 1)])
                P.step("scalar", lambda e: e.activation(
                    rstd_c, var_c, AF.Ln, bias=epscol), wait=[ch_st])
                P.step("scalar", lambda e: e.activation(
                    rstd_c, rstd_c, AF.Exp, scale=nhalf), sig=[(ch_st, 1)])
                P.step("vector", lambda e, xi=xt[sl]: e.tensor_scalar(
                    htile, xi, negm_c, rstd_c, op0=ALU.add, op1=ALU.mult),
                    wait=[ch_st], sig=[(ch_h, 1), (ch_xf[sl], 1)])
                for g in range(ngrp):
                    hf = tp_half()
                    war = tp_state["last"][hf]
                    for j in range(8):
                        cc = g * 8 + j
                        P.step("tensor", lambda e, ci=cc, j2=j, h2=hf: e.transpose(
                            ps_t[:, h2 * 1024 + 128 * j2:h2 * 1024 + 128 * (j2 + 1)],
                            htile[:, 128 * ci:128 * (ci + 1)], id_bf[:, :]),
                            wait=([ch_h, (cst, CST)] + ([war] if war else [])) if j == 0 else (),
                            sig=[(ch_tp, 1)] if j == 7 else ())
                    dst = hT.rearrange("p (ch t) -> p ch t", t=c.TOK)
                    P.step("vector", lambda e, g2=g, h2=hf, d=dst, t0=tt: e.tensor_copy(
                        d[:, g2 * 8:g2 * 8 + 8, t0 * 128:(t0 + 1) * 128],
                        ps_t[:, h2 * 1024:(h2 + 1) * 1024].rearrange(
                            "p (b t) -> p b t", t=128)),
                        wait=[ch_tp], sig=[(ch_ev, 1)], dep=False)
                    tp_state["last"][hf] = (ch_ev, ch_ev.n)
            return ch_ev

        def emit_qk_ln(stage_dram, wcol, bcol_g, row0, gate):
            ktile = scr.take(c.D, BF16)
            ntile = scr.take(c.D, BF16)
            otile = scr.take(c.DCH * 128, BF16)
            ch_x = P.chan("qk_x")
            ch_st = P.chan("qk_st")
            ch_n = P.chan("qk_n")
            ch_tp = P.chan("qk_tp")
            ch_ev = P.chan("qk_ev")
            ch_sd = P.chan("qk_sd")
            ngrp = c.DCH // 8
            for tt in range(c.TT):
                src = stage_dram[tt * 128:(tt + 1) * 128, :]
                dma("sync", ktile, src,
                    wait=list(gate) + ([ch_n] if ch_n.n > 0 else []),
                    sig=[(ch_x, 16)])
                nseg = c.D // 512
                for sgi in range(nseg):
                    P.step("vector", lambda e, s=sgi: e.bn_stats(
                        statw[:, 6 * s:6 * s + 6], ktile[:, 512 * s:512 * (s + 1)]),
                        wait=[ch_x] if sgi == 0 else (), dep=(sgi == 0))
                P.step("vector", lambda e, n=nseg: e.bn_aggr(
                    stat2[:, 0:2], statw[:, 0:6 * n]))
                P.step("vector", lambda e: e.tensor_scalar_mul(negm_c, mean_c, -1.0),
                       sig=[(ch_st, 1)])
                P.step("scalar", lambda e: e.activation(
                    rstd_c, var_c, AF.Ln, bias=epscol), wait=[ch_st])
                P.step("scalar", lambda e: e.activation(
                    rstd_c, rstd_c, AF.Exp, scale=nhalf), sig=[(ch_st, 1)])
                P.step("vector", lambda e: e.tensor_scalar(
                    ntile, ktile, negm_c, rstd_c, op0=ALU.add, op1=ALU.mult),
                    wait=[ch_st, (ch_tp, ch_tp.n)], sig=[(ch_n, 1)])
                for g in range(ngrp):
                    hf = tp_half()
                    war = tp_state["last"][hf]
                    for j in range(8):
                        cc = g * 8 + j
                        P.step("tensor", lambda e, ci=cc, j2=j, h2=hf: e.transpose(
                            ps_t[:, h2 * 1024 + 128 * j2:h2 * 1024 + 128 * (j2 + 1)],
                            ntile[:, 128 * ci:128 * (ci + 1)], id_bf[:, :]),
                            wait=([ch_n] + ([war] if war else [])) if j == 0 else (),
                            sig=[(ch_tp, 1)] if j == 7 else ())
                    for j in range(8):
                        cc = g * 8 + j
                        P.step("vector", lambda e, ci=cc, j2=j, h2=hf: e.tensor_scalar(
                            otile[:, 128 * ci:128 * (ci + 1)],
                            ps_t[:, h2 * 1024 + 128 * j2:h2 * 1024 + 128 * (j2 + 1)],
                            wcol[:, ci:ci + 1], bcol_g[:, ci:ci + 1],
                            op0=ALU.mult, op1=ALU.add),
                            wait=([ch_tp] + ([(ch_sd, ch_sd.n)] if g == 0 else []))
                                 if j == 0 else (),
                            sig=[(ch_ev, 1)] if j == 7 else (), dep=False)
                    tp_state["last"][hf] = (ch_ev, ch_ev.n)
                dstv = agkq_in.rearrange("(r p) t -> p r t", p=128)
                P.step("sync", lambda e, t0=tt, d=dstv: e.dma_start(
                    out=d[:, row0 // 128:row0 // 128 + c.DCH,
                          t0 * 128:(t0 + 1) * 128],
                    in_=otile.rearrange("p (r t) -> p r t", t=128)),
                    wait=[ch_ev], sig=[(ch_sd, 16)])
            return ch_sd

        cv.push()
        hT = cv.take(c.DCH * c.TOK, BF16)
        scr.off = SCR_BASE
        ch_h1 = emit_ln_transpose(x_in, hT, gate=[(cst, CST), (ld, LDF)])
        H1EV = ch_h1.n


        if upto < 2:
            P.replay(blk)
            return nc, P
        wslab = cv.take(2 * c.DCH * 512, BF16)
        stg = cv.take(2 * c.TT * 512, BF16)
        ch_wl = [P.chan(f"wl{i}") for i in range(2)]
        ch_wf = [P.chan(f"wf{i}") for i in range(2)]
        ch_ps = [P.chan(f"ps{i}") for i in range(4)]
        ch_pf = [P.chan(f"pf{i}") for i in range(4)]
        ch_sg = [P.chan(f"sg{i}") for i in range(2)]
        ch_sdq = P.chan("sdq")
        ch_sdk = P.chan("sdk")
        ch_sdv = P.chan("sdv")
        stage_last = [None, None]

        NEB = 3 * c.EB1
        cc_k = P.chan("cc_k")
        cc_q = P.chan("cc_q")
        cc_v = P.chan("cc_v")
        wsrc = wqkvT.rearrange("(ch p) e -> p ch e", p=128)
        ev_per_tile = c.DCH // 8

        def wslab_half(slot):
            return wslab[:, slot * c.DCH * 512:(slot + 1) * c.DCH * 512]

        def load_slab(src_view, col0, target, wait, sigch):
            segs = max(1, c.DCH // 8)
            per = c.DCH // segs
            for sgi in range(segs):
                P.step("gpsimd", lambda e, s=sgi, sv=src_view, t=target, c0=col0, p=per: e.dma_start(
                    out=t.rearrange("p (ch k) -> p ch k", k=512)[:, s * p:(s + 1) * p, :],
                    in_=sv[:, s * p:(s + 1) * p, c0:c0 + 512]),
                    wait=wait if sgi == 0 else (), sig=[(sigch, 16)], isdma=True)
            return 16 * segs

        eb_order = (list(range(c.EB1, 2 * c.EB1)) +
                    list(range(2 * c.EB1, 3 * c.EB1)) +
                    list(range(0, c.EB1)))
        for idx, eb in enumerate(eb_order):
            sl = idx % 2
            w = [ch_wf[sl]] if idx >= 2 else []
            load_slab(wsrc, eb * 512, wslab_half(sl), w, ch_wl[sl])
            for tt in range(c.TT):
                pslot = tt % 4
                for ci in range(c.DCH):
                    wl = []
                    if ci == 0:
                        wl = [ch_wl[sl], (ch_h1, (tt + 1) * ev_per_tile), ch_pf[pslot]]
                    sg = []
                    if ci == c.DCH - 1:
                        sg = [(ch_ps[pslot], 1)]
                        if tt == c.TT - 1:
                            sg.append((ch_wf[sl], 1))
                    P.step("tensor", lambda e, p2=pslot, s=sl, c2=ci, t2=tt: e.matmul(
                        ps_mm[p2],
                        hT.rearrange("p (ch t) -> p ch t", t=c.TOK)[:, c2,
                                     t2 * 128:(t2 + 1) * 128],
                        wslab_half(s).rearrange("p (ch k) -> p ch k", k=512)[:, c2, :],
                        start=(c2 == 0), stop=(c2 == c.DCH - 1)),
                        wait=wl, sig=sg)
                P.step("vector", lambda e, p2=pslot, s=sl, t2=tt: e.tensor_scalar(
                    stg[:, (s * c.TT + t2) * 512:(s * c.TT + t2 + 1) * 512],
                    ps_mm[p2], -c.CLIP, c.CLIP, op0=ALU.max, op1=ALU.min),
                    wait=[ch_ps[pslot]] + ([stage_last[sl]] if (tt == 0 and stage_last[sl] is not None) else []),
                    sig=[(ch_pf[pslot], 1), (ch_sg[sl], 1)], dep=False)
            sec = eb // c.EB1
            ebr = eb % c.EB1
            stg_v = stg[:, sl * c.TT * 512:(sl + 1) * c.TT * 512].rearrange(
                "p (t k) -> p t k", k=512)
            if sec == 0:
                dst = qstage.rearrange("(t p) e -> p t e", p=128)[:, :, ebr * 512:(ebr + 1) * 512]
            elif sec == 1:
                dst = kstage.rearrange("(t p) e -> p t e", p=128)[:, :, ebr * 512:(ebr + 1) * 512]
            else:
                dst = agv_in.rearrange("(t p) e -> p t e", p=128)[:, :, ebr * 512:(ebr + 1) * 512]
            ch_sec = ch_sdq if sec == 0 else (ch_sdk if sec == 1 else ch_sdv)
            P.step("sync", lambda e, d=dst, sv=stg_v: e.dma_start(out=d, in_=sv),
                   wait=[ch_sg[sl]], sig=[(ch_sec, 16)])
            stage_last[sl] = ch_sec
            if idx == c.EB1 - 1:
                scr.off = SCR_BASE
                ch_k_sd = emit_qk_ln(kstage, klnw, klnb, 0, [ch_sdk])
                P.step("gpsimd", lambda e: e.collective_compute(
                    "AllGather", ALU.bypass, replica_groups=RG,
                    ins=[agkq_in[0:c.D, :]], outs=[agk_out[:, :]]),
                    wait=[ch_k_sd], sig=[(cc_k, 1)], isdma=True)
            if idx == 2 * c.EB1 - 1:
                P.step("gpsimd", lambda e: e.collective_compute(
                    "AllGather", ALU.bypass, replica_groups=RG,
                    ins=[agv_in[:, :]], outs=[agv_out[:, :]]),
                    wait=[ch_sdv], sig=[(cc_v, 1)], isdma=True)


        if upto < 3:
            P.replay(blk)
            return nc, P
        ch_q_sd = emit_qk_ln(qstage, qlnw, qlnb, c.D, [ch_sdq])
        P.step("gpsimd", lambda e: e.collective_compute(
            "AllGather", ALU.bypass, replica_groups=RG,
            ins=[agkq_in[c.D:2 * c.D, :]], outs=[agq_out[:, :]]),
            wait=[ch_q_sd], sig=[(cc_q, 1)], isdma=True)


        if upto < 5:
            P.replay(blk)
            return nc, P
        cv.pop()
        cv.push()
        VS = (c.S // 128) * 256
        pairKQ = cv.take(2 * 4 * c.S, BF16)
        pairV = cv.take(2 * VS, BF16)
        sbuf_Sb2 = [cv.take(c.S, F32), cv.take(c.S, F32)]
        pbuf = cv.take(c.KCH, BF16)
        ptbuf = cv.take(2 * c.KCH, BF16)
        astage2 = [cv.take(128, BF16), cv.take(128, BF16)]
        rlrow2 = [rlrow[0:1, 0:128], cv.take(128, F32)[0:1, 0:128]]
        rlb2 = [rlb, cv.take(128, F32)]
        mstage2 = [cv.take(4, F32), cv.take(4, F32)]
        accst2 = [cv.take(4, F32), cv.take(4, F32)]

        lcol2 = [tcol0[:, 0:1], tcol0[:, 8:9]]
        mrun2 = [tcol0[:, 1:2], tcol0[:, 9:10]]
        bcol2 = [tcol0[:, 5:6], tcol0[:, 13:14]]
        rlc2 = [tcol0[:, 6:7], tcol0[:, 14:15]]

        scr.off = SCR_BASE
        r0full = scr.take(c.S, F32)
        ch_r0 = P.chan("p5_r0")
        P.step("gpsimd", lambda e: e.iota(
            r0full[:, :], [[1, c.S]], base=-(c.S - 128), channel_multiplier=-1,
            allow_small_or_imprecise_dtypes=True),
            wait=[cc_k, cc_q, cc_v], sig=[(ch_r0, 1)], dep=False)

        ch_pl = [P.chan(f"p5_pl{i}") for i in range(2)]
        ch_pfree = [P.chan(f"p5_pfree{i}") for i in range(2)]
        ch_pes = [P.chan(f"p5_pes{i}") for i in range(2)]
        ch_dsb = [P.chan(f"p5_dsb{i}") for i in range(2)]
        ch_gmx = P.chan("p5_gmx")
        ch_msk = P.chan("p5_msk")
        ch_exp = P.chan("p5_exp")
        ch_ptp = P.chan("p5_ptp")
        ch_pte = P.chan("p5_pte")
        ch_pvg = P.chan("p5_pvg")
        ch_oev = P.chan("p5_oev")
        ch_pvd = P.chan("p5_pvd")
        ch_fin = P.chan("p5_fin")
        ch_adr = P.chan("p5_adr")

        gk = agk_out.rearrange("(i rb hl p) t -> i rb hl p t", rb=c.G, hl=c.HL, p=128)
        gq = agq_out.rearrange("(i rb hl p) t -> i rb hl p t", rb=c.G, hl=c.HL, p=128)
        gv = agv_out.rearrange("(kt p) (rb gp q) -> p kt rb gp q", p=128, rb=c.G, q=256)
        pp_glob = [0]
        exp_qt_end = [0, 0]
        gmx_prev = [None, None]
        ladtag = [None, None]
        rlwar = [None, None]

        def kq_col(slot, hh, kq, k):
            return (slot * 4 * c.S + hh * 2 * c.S +
                    (k // c.TOK) * 2 * c.TOK + kq * c.TOK + (k % c.TOK))

        def kq_sl(slot, hh, kq, k, w):
            b = kq_col(slot, hh, kq, k)
            if (k % c.TOK) + w <= c.TOK:
                return pairKQ[:, b:b + w]
            assert k % c.TOK == 0 and w % c.TOK == 0
            nb = w // c.TOK
            return pairKQ[:, b:b + (nb - 1) * 2 * c.TOK + c.TOK].rearrange(
                "p (b2 t) -> p b2 t", t=c.TOK)[:, ::2, :] if False else bass.AP(
                tensor=pairKQ.tensor, offset=pairKQ.offset + b,
                ap=[list(pairKQ.ap[0]), [2 * c.TOK, nb], [1, c.TOK]])
        ptlast = [None, None]

        pend5 = [None]

        def emit_passA(ctx):
            sl, hh, qt, qp = ctx["sl"], ctx["hh"], ctx["qt"], ctx["qp"]
            scol = ctx["scol"]
            Sb = sbuf_Sb2[qp]
            mst = mstage2[qp]
            klen = 128 * (qt + 1)
            nch = (klen + c.KCH - 1) // c.KCH
            ctx["nch"] = nch
            ctx["nkt"] = klen // 128
            ctx["klen"] = klen
            atags = []
            for chi in range(nch):
                k0 = chi * c.KCH
                w = min(c.KCH, klen - k0)
                pp = pp_glob[0] % 2
                pp_glob[0] += 1
                ctx.setdefault("pps", []).append(pp)
                pss = ps_s0 if pp == 0 else ps_s1
                nmm = (w + 511) // 512
                for j in range(nmm):
                    wj = min(512, w - j * 512)
                    wl = []
                    if j == 0:
                        wl = [ch_dsb[pp]]
                        if ctx["first"]:
                            wl.append(ch_pl[sl])
                    qsl = kq_sl(sl, hh, 1, qt * 128, 128)
                    ksl = kq_sl(sl, hh, 0, k0 + j * 512, wj)
                    P.step("tensor", lambda e, j2=j, w2=wj,
                           ps=pss, kh=ksl, qh=qsl: e.matmul(
                        ps[:, j2 * 512:j2 * 512 + w2],
                        qh, kh, start=True, stop=True),
                        wait=wl,
                        sig=[(ch_pes[pp], 1)] if j == nmm - 1 else ())
                c0 = c.S - 128 * (qt + 1) + k0
                wl = [ch_pes[pp], (ch_exp, exp_qt_end[qp])]
                if ctx["first"] and chi == 0 and sl == 0 and ch_r0.n == 1:
                    wl.append((ch_r0, 1))
                P.step("vector", lambda e, w2=w, k2=k0, c2=c0, ps=pss, sc=scol,
                       sb=Sb: e.scalar_tensor_tensor(
                    sb[:, k2:k2 + w2], r0full[:, c2:c2 + w2], sc,
                    ps[:, 0:w2], op0=ALU.mult, op1=ALU.add),
                    wait=wl, sig=[(ch_dsb[pp], 1)], dep=False)
                atags.append((ch_dsb[pp], ch_dsb[pp].n))
            P.step("gpsimd", lambda e, kl=klen, sb=Sb: e.affine_select(
                sb[:, kl - 128:kl], sb[:, kl - 128:kl],
                [[-1, 128]], ALU.is_ge, fillreg(e, NEG_BIG),
                base=0, channel_multiplier=1),
                wait=[atags[-1]], sig=[(ch_msk, 1)], dep=False)
            msk_tag = (ch_msk, ch_msk.n)
            for chi in range(nch):
                k0 = chi * c.KCH
                w = min(c.KCH, klen - k0)
                wl = [atags[chi]]
                if chi == nch - 1:
                    wl.append(msk_tag)
                if chi == 0 and gmx_prev[qp] is not None:
                    wl.append(gmx_prev[qp])
                P.step("vector", lambda e, w2=w, k2=k0, sb=Sb, ci=chi, m=mst:
                       e.reduce_max(m[:, ci:ci + 1], sb[:, k2:k2 + w2], axis=AX.X),
                       wait=wl, dep=False)
            if nch > 1:
                P.step("vector", lambda e, m=mst, n2=nch, mr=mrun2[qp]:
                       e.reduce_max(mr, m[:, 0:n2], axis=AX.X))
                msrc = mrun2[qp]
            else:
                msrc = mst[:, 0:1]
            P.step("vector", lambda e, b=bcol2[qp], m=msrc: e.tensor_scalar_mul(
                b, m, -1.0),
                wait=[(ch_exp, exp_qt_end[qp])], sig=[(ch_gmx, 1)])
            gmx_prev[qp] = (ch_gmx, ch_gmx.n)
            ctx["gmx_end"] = ch_gmx.n

        def emit_passB(ctx):
            sl, hh, qt, qp = ctx["sl"], ctx["hh"], ctx["qt"], ctx["qp"]
            hloc = ctx["hloc"]
            nch, nkt, klen = ctx["nch"], ctx["nkt"], ctx["klen"]
            Sb = sbuf_Sb2[qp]
            acc = accst2[qp]
            ktg = 0
            expt = []
            for chi in range(nch):
                k0 = chi * c.KCH
                w = min(c.KCH, klen - k0)
                nkt_ch = w // 128
                pslot = chi % 2
                wl = [(ch_gmx, ctx["gmx_end"]), (ch_ptp, ch_ptp.n)]
                if chi == 0 and ladtag[qp] is not None:
                    wl.append(ladtag[qp])
                P.step("scalar", lambda e, w2=w, k2=k0, sb=Sb, ci=chi, a=acc,
                       b=bcol2[qp]: e.activation(
                    pbuf[:, 0:w2], sb[:, k2:k2 + w2], AF.Exp,
                    bias=b, accum_out=a[:, ci:ci + 1]),
                    wait=wl, sig=[(ch_exp, 1)], dep=False)
                expt.append((ch_exp, ch_exp.n))
                hf = tp_half()
                war = tp_state["last"][hf]
                for j in range(nkt_ch):
                    P.step("tensor", lambda e, k2=j, h2=hf: e.transpose(
                        ps_t[:, h2 * 1024 + k2 * 128:h2 * 1024 + (k2 + 1) * 128],
                        pbuf[:, k2 * 128:(k2 + 1) * 128], id_bf[:, :]),
                        wait=([expt[-1]] + ([war] if war else [])) if j == 0 else (),
                        sig=[(ch_ptp, 1)] if j == nkt_ch - 1 else ())
                P.step("scalar", lambda e, h2=hf, n2=nkt_ch, p2=pslot: e.activation(
                    ptbuf[:, p2 * c.KCH:p2 * c.KCH + n2 * 128],
                    ps_t[:, h2 * 1024:h2 * 1024 + n2 * 128], AF.Copy),
                    wait=[(ch_ptp, ch_ptp.n)] + ([ptlast[pslot]] if ptlast[pslot] else []),
                    sig=[(ch_pte, 1)], dep=False)
                tp_state["last"][hf] = (ch_pte, ch_pte.n)
                for j in range(nkt_ch):
                    kg = ktg + j
                    wl = [ch_pte] if j == 0 else []
                    if kg == 0:
                        wl.append(ch_oev)
                    sg = []
                    if j == nkt_ch - 1:
                        sg.append((ch_pvg, 1))
                    if kg == nkt - 1:
                        sg.append((ch_pvd, 1))
                        if qt == c.QT - 1 and hh == 1:
                            sg.append((ch_pfree[sl], 1))
                    P.step("tensor", lambda e, kg2=kg, j2=j, s=sl, h3=hh, p2=pslot, nk=nkt: e.matmul(
                        ps_o[:, 0:128],
                        pairV[:, s * VS:(s + 1) * VS].rearrange(
                            "p (kt q) -> p kt q", q=256)[:, kg2,
                            h3 * 128:(h3 + 1) * 128],
                        ptbuf[:, p2 * c.KCH + j2 * 128:p2 * c.KCH + (j2 + 1) * 128],
                        start=(kg2 == 0), stop=(kg2 == nk - 1)),
                        wait=wl, sig=sg)
                ptlast[pslot] = (ch_pvg, ch_pvg.n)
                ktg += nkt_ch
            exp_qt_end[qp] = ch_exp.n
            if nch > 1:
                wl = [expt[-1]] + ([rlwar[qp]] if rlwar[qp] else [])
                P.step("vector", lambda e, l=lcol2[qp], a=acc, n2=nch: e.reduce_sum(
                    l, a[:, 0:n2], axis=AX.X), wait=wl, dep=False)
                P.step("vector", lambda e, r=rlc2[qp], l=lcol2[qp]: e.reciprocal(
                    r, l), sig=[(ch_fin, 1)])
            else:
                wl = [expt[-1]] + ([rlwar[qp]] if rlwar[qp] else [])
                P.step("vector", lambda e, r=rlc2[qp], a=acc: e.reciprocal(
                    r, a[:, 0:1]), wait=wl, sig=[(ch_fin, 1)], dep=False)
            ladtag[qp] = (ch_fin, ch_fin.n)
            P.step("tensor", lambda e, r=rlc2[qp]: e.transpose(
                ps_w[0:1, 0:128], r, id_f32[:, :]),
                wait=[ch_fin], sig=[(ch_fin, 1)])
            P.step("vector", lambda e, rr=rlrow2[qp]: e.tensor_copy(
                rr, ps_w[0:1, 0:128]),
                wait=[ch_fin], sig=[(ch_fin, 1)], dep=False)
            P.step("tensor", lambda e, rr=rlrow2[qp]: e.matmul(
                ps_w[:, 128:256], ones_f32[0:1, :], rr,
                start=True, stop=True), wait=[ch_fin], sig=[(ch_fin, 1)])
            P.step("vector", lambda e, rb=rlb2[qp]: e.tensor_copy(
                rb, ps_w[:, 128:256]),
                wait=[ch_fin], sig=[(ch_fin, 1)], dep=False)
            rlwar[qp] = (ch_fin, ch_fin.n)
            P.step("vector", lambda e, a=astage2[qp], rb=rlb2[qp]: e.tensor_tensor(
                a, ps_o[:, 0:128], rb, ALU.mult),
                wait=[ch_fin, ch_pvd, (ch_adr, ch_adr.n)],
                sig=[(ch_oev, 1)], dep=False)
            adst = agat_in.rearrange("(j h p) t -> p j h t", p=128, h=c.HL)
            P.step("sync", lambda e, q2=qt, h2=hloc, d=adst, a=astage2[qp]: e.dma_start(
                out=d[:, q2 // c.TT, h2, (q2 % c.TT) * 128:(q2 % c.TT + 1) * 128],
                in_=a),
                wait=[ch_oev], sig=[(ch_adr, 16)], isdma=True)

        npair = c.HL // 2
        for pr in range(npair):
            sl = pr % 2
            wl0 = [cc_k, cc_q, cc_v] if pr == 0 else [ch_pfree[sl]]
            for hh in range(2):
                for kq, gsrc in ((0, gk), (1, gq)):
                    o3 = pairKQ[:, sl * 4 * c.S + hh * 2 * c.S:
                                sl * 4 * c.S + (hh + 1) * 2 * c.S].rearrange(
                        "p (i kq t) -> p kq i t", kq=2, t=c.TOK)[:, kq]
                    P.step("gpsimd", lambda e, p2=pr, h2=hh, o=o3, g3=gsrc: e.dma_start(
                        out=o,
                        in_=g3[:, rankof(e), p2 * 2 + h2].transpose([1, 0, 2])),
                        wait=wl0 if (hh == 0 and kq == 0) else (),
                        sig=[(ch_pl[sl], 16)], isdma=True)
            P.step("gpsimd", lambda e, s=sl, p2=pr: e.dma_start(
                out=pairV[:, s * VS:(s + 1) * VS].rearrange("p (kt q) -> p kt q", q=256),
                in_=gv[:, :, rankof(e), p2, :]),
                sig=[(ch_pl[sl], 16)], isdma=True)

            for hh in range(2):
                hloc = pr * 2 + hh
                for qt in range(c.QT):
                    cur = dict(sl=sl, hh=hh, hloc=hloc, qt=qt, qp=qt % 2,
                               scol=slope_col[:, hloc:hloc + 1],
                               first=(qt == 0 and hh == 0))
                    emit_passA(cur)
                    if pend5[0] is not None:
                        emit_passB(pend5[0])
                    pend5[0] = cur
            if pend5[0] is not None:
                emit_passB(pend5[0])
                pend5[0] = None

        cc_at = P.chan("cc_at")
        P.step("gpsimd", lambda e: e.collective_compute(
            "AllGather", ALU.bypass, replica_groups=RG,
            ins=[agat_in[:, :]], outs=[agat_out[:, :]]),
            wait=[ch_adr], sig=[(cc_at, 1)], isdma=True)
        cv.pop()


        if upto < 6:
            P.replay(blk)
            return nc, P
        cv.push()
        attnT = cv.take(c.DCH * c.TOK, BF16)
        wslab6 = cv.take(2 * c.DCH * 512, BF16)
        xtl = [cv.take(512, F32), cv.take(512, F32)]
        x1s = [cv.take(512, F32), cv.take(512, F32)]

        ch_atl = P.chan("p6_atl")
        ch_xl = [P.chan(f"xl{i}") for i in range(2)]
        ch_xf = [P.chan(f"xf{i}") for i in range(2)]
        ch_ev = [P.chan(f"ev{i}") for i in range(2)]
        ch_x1sd = P.chan("p6_x1sd")

        x1dma_tag = [0, 0]
        gat = agat_out.rearrange("(i j h p) t -> i j h p t", j=c.G, h=c.HL, p=128)
        for i in range(c.G):
            P.step("gpsimd", lambda e, i2=i: e.dma_start(
                out=attnT.rearrange("p (i h t) -> p i h t", i=c.G, h=c.HL)[:, i2],
                in_=gat[i2, rankof(e)].transpose([1, 0, 2])),
                wait=[cc_at] if i == 0 else (), sig=[(ch_atl, 16)], isdma=True)
        ATL = ch_atl.n

        wsrc6 = woutT.rearrange("(ch p) e -> p ch e", p=128)

        def wslab6_half(slot):
            return wslab6[:, slot * c.DCH * 512:(slot + 1) * c.DCH * 512]

        for eb in range(c.EB1):
            sl = eb % 2
            w = [ch_wf[sl]] + ([cc_at] if eb < 2 else [])
            segs = max(1, c.DCH // 8)
            per = c.DCH // segs
            for sgi in range(segs):
                P.step("gpsimd", lambda e, s=sgi, c0=eb * 512, t=wslab6_half(sl), p=per: e.dma_start(
                    out=t.rearrange("p (ch k) -> p ch k", k=512)[:, s * p:(s + 1) * p, :],
                    in_=wsrc6[:, s * p:(s + 1) * p, c0:c0 + 512]),
                    wait=w if sgi == 0 else (), sig=[(ch_wl[sl], 16)], isdma=True)
            for tt in range(c.TT):
                pslot = tt % 4
                xsl = tt % 2
                dma("sync", xtl[xsl], x_in[tt * 128:(tt + 1) * 128, eb * 512:(eb + 1) * 512],
                    wait=([ch_xf[xsl]] if ch_xf[xsl].n > 0 else []) +
                         ([cc_at] if eb == 0 and tt < 2 else []),
                    sig=[(ch_xl[xsl], 16)])
                for ci in range(c.DCH):
                    wl = []
                    if ci == 0:
                        wl = [ch_wl[sl], (ch_atl, ATL), ch_pf[pslot]]
                    sg = []
                    if ci == c.DCH - 1:
                        sg = [(ch_ps[pslot], 1)]
                        if tt == c.TT - 1:
                            sg.append((ch_wf[sl], 1))
                    P.step("tensor", lambda e, p2=pslot, s=sl, c2=ci, t2=tt: e.matmul(
                        ps_mm[p2],
                        attnT.rearrange("p (ch t) -> p ch t", t=c.TOK)[:, c2,
                                        t2 * 128:(t2 + 1) * 128],
                        wslab6_half(s).rearrange("p (ch k) -> p ch k", k=512)[:, c2, :],
                        start=(c2 == 0), stop=(c2 == c.DCH - 1)),
                        wait=wl, sig=sg)
                P.step("vector", lambda e, p2=pslot, xs=xsl: e.tensor_tensor(
                    x1s[xs], ps_mm[p2], xtl[xs], ALU.add),
                    wait=[ch_ps[pslot], ch_xl[xsl], (ch_x1sd, x1dma_tag[xsl])],
                    sig=[(ch_pf[pslot], 1), (ch_xf[xsl], 1), (ch_ev[xsl], 1)],
                    dep=False)
                P.step("sync", lambda e, t2=tt, e2=eb, xs=xsl: e.dma_start(
                    out=x1d[t2 * 128:(t2 + 1) * 128, e2 * 512:(e2 + 1) * 512],
                    in_=x1s[xs]),
                    wait=[ch_ev[xsl]], sig=[(ch_x1sd, 16)])
                x1dma_tag[xsl] = ch_x1sd.n
        X1SD = ch_x1sd.n
        cv.pop()


        if upto < 7:
            P.replay(blk)
            return nc, P
        cv.push()
        h2T = cv.take(c.DCH * c.TOK, BF16)
        scr.off = SCR_BASE
        ch_h2 = emit_ln_transpose(x1d, h2T, gate=[(ch_x1sd, X1SD)])
        H2EV = ch_h2.n
        LTP = P.chan("lt_tp").n


        if upto < 8:
            P.replay(blk)
            return nc, P
        dslab8 = cv.take(2 * c.RC * 512, BF16)
        bslab8 = cv.take(2 * c.DCH * 512, BF16)
        scr.off = SCR_BASE
        basebrow = scr.take(c.D, BF16)
        atslab = scr.take(c.DCH * 64, BF16)
        rwslab = scr.take(c.DCH * 16, BF16)
        resTd = scr.take(c.TOK, F32)
        sstack = scr.take(c.RC * c.TOK, BF16)
        combT = scr.take(128, F32)
        crow = [scr.take(c.TOK, F32), scr.take(c.TOK, F32)]
        lgt = scr.take(16, F32)
        prb = scr.take(16, F32)
        xx2 = scr.take(16, F32)
        eq1 = scr.take(16, F32)
        eq2 = scr.take(16, F32)
        comb = scr.take(16, F32)
        xtl8 = [scr.take(512, F32), scr.take(512, F32)]
        outs8 = [scr.take(512, F32), scr.take(512, F32)]
        rcol = tcol0[:, 8:9]
        m1c = tcol0[:, 9:10]
        m2c = tcol0[:, 10:11]
        denc = tcol0[:, 11:12]
        w1c = tcol0[:, 12:13]
        w2c = tcol0[:, 13:14]
        nm1c = tcol0[:, 14:15]
        sumc = tcol0[:, 15:16]

        ch_cl8 = P.chan("p8_cl")
        ch_lg = P.chan("p8_lg")
        ch_rt = P.chan("p8_rt")
        ch_cmb = P.chan("p8_cmb")
        ch_res = P.chan("p8_res")
        ch_sst = P.chan("p8_sst")
        ch_out = P.chan("out_dma")

        atsrc = at_in.rearrange("(ch p) r -> p ch r", p=128)
        rwsrc = rwt_in.rearrange("(ch p) r -> p ch r", p=128)
        ch_ltp = P.chan("lt_tp")
        P.step("gpsimd", lambda e: e.dma_start(
            out=atslab.rearrange("p (ch r) -> p ch r", r=64), in_=atsrc),
            wait=[(ch_ltp, LTP)], sig=[(ch_cl8, 16)], isdma=True)
        P.step("gpsimd", lambda e: e.dma_start(
            out=rwslab.rearrange("p (ch r) -> p ch r", r=16), in_=rwsrc),
            sig=[(ch_cl8, 16)], isdma=True)
        dma("gpsimd", basebrow[0:1, :], baseb_in, sig=[(ldg, 16)])

        for tt in range(c.TT):
            for ci in range(c.DCH):
                P.step("tensor", lambda e, c2=ci, t2=tt: e.matmul(
                    ps_w[:, 0:16],
                    h2T.rearrange("p (ch t) -> p ch t", t=c.TOK)[:, c2,
                                  t2 * 128:(t2 + 1) * 128],
                    rwslab.rearrange("p (ch r) -> p ch r", r=16)[:, c2, :],
                    start=(c2 == 0), stop=(c2 == c.DCH - 1)),
                    wait=[(ch_cl8, 32), (ch_h2, H2EV), (ch_rt, ch_rt.n)] if ci == 0 else (),
                    sig=[(ch_lg, 1)] if ci == c.DCH - 1 else ())
            P.step("vector", lambda e: e.tensor_copy(lgt, ps_w[:, 0:16]),
                   wait=[ch_lg], sig=[(ch_rt, 1)])
            P.step("vector", lambda e: e.reduce_max(m1c, lgt, axis=AX.X))
            P.step("vector", lambda e: e.tensor_scalar_mul(nm1c, m1c, -1.0),
                   sig=[(ch_rt, 1)])
            P.step("scalar", lambda e: e.activation(
                prb, lgt, AF.Exp, bias=nm1c, accum_out=sumc),
                wait=[ch_rt], sig=[(ch_rt, 1)])
            P.step("vector", lambda e: e.reciprocal(rcol, sumc), wait=[ch_rt])
            P.step("vector", lambda e: e.tensor_scalar(
                prb, prb, rcol, None, op0=ALU.mult))
            P.step("vector", lambda e: e.reduce_max(m1c, prb, axis=AX.X))
            P.step("vector", lambda e: e.tensor_scalar(
                eq1, prb, m1c, None, op0=ALU.is_equal))
            P.step("vector", lambda e: e.tensor_scalar_mul(nm1c, m1c, -1.0))
            P.step("vector", lambda e: e.scalar_tensor_tensor(
                xx2, eq1, nm1c, prb, op0=ALU.mult, op1=ALU.add))
            P.step("vector", lambda e: e.reduce_max(m2c, xx2, axis=AX.X))
            P.step("vector", lambda e: e.tensor_add(denc, m1c, m2c))
            P.step("vector", lambda e: e.tensor_scalar(
                denc, denc, 1.0e-6, None, op0=ALU.add))
            P.step("vector", lambda e: e.reciprocal(denc, denc))
            P.step("vector", lambda e: e.tensor_mul(w1c, m1c, denc))
            P.step("vector", lambda e: e.tensor_mul(w2c, m2c, denc))
            P.step("vector", lambda e: e.tensor_scalar(
                eq2, xx2, m2c, None, op0=ALU.is_equal))
            P.step("vector", lambda e: e.tensor_scalar(
                eq2, eq2, w2c, None, op0=ALU.mult))
            P.step("vector", lambda e: e.scalar_tensor_tensor(
                comb, eq1, w1c, eq2, op0=ALU.mult, op1=ALU.add),
                sig=[(ch_rt, 1)])
            P.step("tensor", lambda e: e.transpose(
                ps_w[0:16, 128:256], comb, id_f32[:, :]),
                wait=[ch_rt, (ch_cmb, ch_cmb.n)], sig=[(ch_rt, 1)])
            P.step("vector", lambda e: e.tensor_copy(
                combT[0:16, 0:128], ps_w[0:16, 128:256]),
                wait=[ch_rt, (ch_cmb, ch_cmb.n)], sig=[(ch_rt, 1), (ch_cmb, 1)])
            P.step("sync", lambda e, t2=tt: e.dma_start(
                out=combT_d[:, t2 * 128:(t2 + 1) * 128], in_=combT[0:16, 0:128]),
                wait=[(ch_cmb, ch_cmb.n)], sig=[(ch_cmb, 16)])
        CMB = ch_cmb.n
        if upto == 81:
            P.replay(blk)
            return nc, P

        NB2 = c.TOK // c.TB
        for tb in range(NB2):
            for half in range(2):
                for ci in range(c.DCH):
                    sg = []
                    if ci == c.DCH - 1 and half == 1:
                        sg = [(ch_res, 1)]
                    P.step("tensor", lambda e, c2=ci, t2=tb, h2=half: e.matmul(
                        ps_o[h2 * 64:(h2 + 1) * 64, 0:c.TB],
                        atslab.rearrange("p (ch r) -> p ch r", r=64)[:, c2, :],
                        h2T.rearrange("p (ch t) -> p ch t", t=c.TOK)[:, c2,
                                      t2 * c.TB:(t2 + 1) * c.TB],
                        start=(c2 == 0), stop=(c2 == c.DCH - 1)),
                        wait=[(ch_cl8, 32), (ch_h2, H2EV), (ch_res, ch_res.n)]
                             if (ci == 0 and half == 0) else (),
                        sig=sg)
            P.step("vector", lambda e, t2=tb: e.tensor_copy(
                resTd[:, t2 * c.TB:(t2 + 1) * c.TB], ps_o[:, 0:c.TB]),
                wait=[ch_res], sig=[(ch_res, 1)], dep=False)
        RESN = ch_res.n
        if upto == 82:
            P.replay(blk)
            return nc, P


        ch_srow = P.chan("p8_srow")
        crow2 = crow[0]
        for rc in range(c.RC):
            P.step("gpsimd", lambda e, e2=rc * 2: e.dma_start(
                out=crow2[0:2, :], in_=combT_d[e2:e2 + 2, :]),
                wait=[(ch_cmb, CMB), (ch_ltp, LTP)] +
                     ([(ch_sst, ch_sst.n)] if rc > 0 else []),
                sig=[(ch_srow, 16)], isdma=True)
            for tb in range(NB2):
                P.step("tensor", lambda e, t2=tb: e.matmul(
                    ps_w[:, 0:c.TB],
                    selm[0:2, :],
                    crow2[0:2, t2 * c.TB:(t2 + 1) * c.TB],
                    start=True, stop=True),
                    wait=[ch_srow, (ch_sst, ch_sst.n)],
                    sig=[(ch_sst, 1)])
                P.step("vector", lambda e, r2=rc, t2=tb: e.tensor_tensor(
                    sstack[:, r2 * c.TOK + t2 * c.TB:r2 * c.TOK + (t2 + 1) * c.TB],
                    ps_w[:, 0:c.TB], resTd[:, t2 * c.TB:(t2 + 1) * c.TB], ALU.mult),
                    wait=[ch_sst, (ch_res, RESN)], sig=[(ch_sst, 1)], dep=False)

        SSTN = ch_sst.n
        if upto == 83:
            P.replay(blk)
            return nc, P


        bsrc = basewT.rearrange("(ch p) e -> p ch e", p=128)
        dsrc = bstk_in.rearrange("(rc p) e -> p rc e", p=128)

        def bslab8_half(slot):
            return bslab8[:, slot * c.DCH * 512:(slot + 1) * c.DCH * 512]

        def dslab8_half(slot):
            return dslab8[:, slot * c.RC * 512:(slot + 1) * c.RC * 512]

        outdma_tag = [0, 0]
        for eb in range(c.EB1):
            sl = eb % 2
            w = [ch_wf[sl]] + ([(ch_ltp, LTP)] if eb < 2 else [])
            segs = max(1, c.DCH // 8)
            per = c.DCH // segs
            for sgi in range(segs):
                P.step("gpsimd", lambda e, s=sgi, c0=eb * 512, t=bslab8_half(sl), p=per: e.dma_start(
                    out=t.rearrange("p (ch k) -> p ch k", k=512)[:, s * p:(s + 1) * p, :],
                    in_=bsrc[:, s * p:(s + 1) * p, c0:c0 + 512]),
                    wait=w if sgi == 0 else (), sig=[(ch_wl[sl], 16)], isdma=True)
            P.step("gpsimd", lambda e, c0=eb * 512, t=dslab8_half(sl): e.dma_start(
                out=t.rearrange("p (rc k) -> p rc k", k=512),
                in_=dsrc[:, :, c0:c0 + 512]),
                sig=[(ch_wl[sl], 16)], isdma=True)
            for tt in range(c.TT):
                pslot = tt % 4
                xsl = tt % 2
                dma("sync", xtl8[xsl], x1d[tt * 128:(tt + 1) * 128, eb * 512:(eb + 1) * 512],
                    wait=[(ch_x1sd, X1SD), (ch_ltp, LTP)] +
                         ([ch_xf[xsl]] if ch_xf[xsl].n > 0 else []),
                    sig=[(ch_xl[xsl], 16)])
                skip_delta = False
                skip_bias = False
                for ci in range(c.DCH):
                    wl = []
                    if ci == 0:
                        wl = [ch_wl[sl], (ch_sst, SSTN), ch_pf[pslot], (ldg, 16)]
                    P.step("tensor", lambda e, p2=pslot, s=sl, c2=ci, t2=tt: e.matmul(
                        ps_mm[p2],
                        h2T.rearrange("p (ch t) -> p ch t", t=c.TOK)[:, c2,
                                      t2 * 128:(t2 + 1) * 128],
                        bslab8_half(s).rearrange("p (ch k) -> p ch k", k=512)[:, c2, :],
                        start=(c2 == 0), stop=(skip_delta and skip_bias and c2 == c.DCH - 1)),
                        wait=wl)
                if not skip_bias:
                    P.step("tensor", lambda e, p2=pslot, e2=eb: e.matmul(
                        ps_mm[p2], onesrow_bf[0:1, :],
                        basebrow[0:1, e2 * 512:(e2 + 1) * 512],
                        start=False, stop=skip_delta))
                sg = [(ch_ps[pslot], 1)]
                if tt == c.TT - 1:
                    sg.append((ch_wf[sl], 1))
                if not skip_delta:
                    for rc in range(c.RC):
                        P.step("tensor", lambda e, p2=pslot, s=sl, r2=rc, t2=tt: e.matmul(
                            ps_mm[p2],
                            sstack.rearrange("p (rc t) -> p rc t", t=c.TOK)[:, r2,
                                             t2 * 128:(t2 + 1) * 128],
                            dslab8_half(s).rearrange("p (rc k) -> p rc k", k=512)[:, r2, :],
                            start=False, stop=(r2 == c.RC - 1)),
                            sig=sg if rc == c.RC - 1 else ())
                else:
                    P.step("tensor", lambda e: e.drain(), sig=sg)
                P.step("vector", lambda e, p2=pslot, xs=xsl: e.tensor_tensor(
                    outs8[xs], ps_mm[p2], xtl8[xs], ALU.add),
                    wait=[ch_ps[pslot], ch_xl[xsl], (ch_out, outdma_tag[xsl])],
                    sig=[(ch_pf[pslot], 1), (ch_xf[xsl], 1), (ch_ev[xsl], 1)],
                    dep=False)
                P.step("sync", lambda e, t2=tt, e2=eb, xs=xsl: e.dma_start(
                    out=y_out[t2 * 128:(t2 + 1) * 128, e2 * 512:(e2 + 1) * 512],
                    in_=outs8[xs]),
                    wait=[ch_ev[xsl]], sig=[(ch_out, 16)])
                outdma_tag[xsl] = ch_out.n

        P.step("sync", None, wait=[(ch_out, ch_out.n)])

        P.replay(blk)

    return nc, P



def _shards(cfg, inputs):
    c = cfg
    x = inputs["L0_attn_hidden_states"]
    wqkvT = np.ascontiguousarray(np.asarray(inputs["L0_attn_w_qkv"], np.float32).T)
    woutT = np.ascontiguousarray(np.asarray(inputs["L0_attn_w_out"], np.float32).T)
    basewT = np.ascontiguousarray(np.asarray(inputs["L0_ffn_base_w"], np.float32).T)
    AT = np.ascontiguousarray(np.asarray(inputs["L0_ffn_A"], np.float32).T)
    RWT = np.ascontiguousarray(np.asarray(inputs["L0_ffn_router_w"], np.float32).T)
    Bstk = np.ascontiguousarray(
        np.asarray(inputs["L0_ffn_B"], np.float32).transpose(0, 2, 1).reshape(c.E * c.R, c.D))
    baseb = np.ascontiguousarray(np.asarray(inputs["L0_ffn_base_b"], np.float32).reshape(1, c.D))
    qlnw = np.ascontiguousarray(np.asarray(inputs["L0_attn_q_ln_w"], np.float32).reshape(c.DCH, 128).T)
    qlnb = np.ascontiguousarray(np.asarray(inputs["L0_attn_q_ln_b"], np.float32).reshape(c.DCH, 128).T)
    klnw = np.ascontiguousarray(np.asarray(inputs["L0_attn_k_ln_w"], np.float32).reshape(c.DCH, 128).T)
    klnb = np.ascontiguousarray(np.asarray(inputs["L0_attn_k_ln_b"], np.float32).reshape(c.DCH, 128).T)
    slopes = np.asarray(alibi_slopes(c.H), np.float32)

    in_maps = []
    for core in range(c.NC):
        g, r = core // c.G, core % c.G
        xs = np.ascontiguousarray(np.asarray(x[g, r * c.TOK:(r + 1) * c.TOK, :], np.float32))
        sl = np.ascontiguousarray(slopes[c.HL * r:c.HL * (r + 1)].reshape(1, c.HL))
        in_maps.append({
            "x": xs, "wqkvT": wqkvT, "woutT": woutT, "basewT": basewT,
            "AT": AT, "RWT": RWT, "Bstk": Bstk, "baseb": baseb,
            "qlnw": qlnw, "qlnb": qlnb, "klnw": klnw, "klnb": klnb,
            "slopes": sl,
        })
    return in_maps


def kernel(**inputs):
    cfg = Cfg()
    nc, _ = build(cfg)
    in_maps = _shards(cfg, inputs)
    res = run_bass_kernel_spmd(nc, in_maps, core_ids=list(range(cfg.NC)))
    y = np.zeros((cfg.B, cfg.S, cfg.D), np.float32)
    for core in range(cfg.NC):
        g, r = core // cfg.G, core % cfg.G
        y[g, r * cfg.TOK:(r + 1) * cfg.TOK, :] = np.asarray(res.results[core]["out"])
    return y


if __name__ == "__main__":
    tc = Cfg(D=1024, S=512, H=8, R=64, E=16)
    nc, P = build(tc)
    print("instruction counts:", P.counts())


# version 11 (final)
# speedup vs baseline: 1.5276x; 1.0272x over previous
import math

import numpy as np

import concourse.bass as bass
import concourse.mybir as mybir
from concourse.bass_utils import run_bass_kernel_spmd

F32 = mybir.dt.float32
BF16 = mybir.dt.bfloat16
AF = mybir.ActivationFunctionType
ALU = mybir.AluOpType
AX = mybir.AxisListType

NEG_BIG = -1.0e30


class Cfg:
    def __init__(self, D=4096, S=4096, H=32, R=64, E=16):
        self.D = D
        self.S = S
        self.B = 2
        self.H = H
        self.HD = 128
        self.NC = 8
        self.G = 4
        self.R = R
        self.E = E
        assert D == H * 128
        self.TOK = S // self.G
        self.TT = self.TOK // 128
        self.DCH = D // 128
        self.QT = S // 128
        self.HL = H // self.G
        self.EB1 = D // 512
        self.KCH = 1024
        self.RC = (E * R) // 128
        self.CLIP = 8.0
        self.EPS = 1.0e-5
        self.SCALING = 32.0 / 64.0
        self.ISQ = 1.0 / math.sqrt(128.0)
        self.R0W = 128 + min(self.KCH, S)
        assert self.HL % 2 == 0 and self.DCH % 8 == 0
        self.TB = min(512, self.TOK)
        assert self.TOK % 128 == 0 and self.R == 64


def alibi_slopes(h, bias_max=8):
    p2 = 2 ** math.ceil(math.log2(h))
    m = [(i + 1) * (bias_max / p2) for i in range(p2)]
    s = [1.0 / (2.0 ** mi) for mi in m]
    if p2 != h:
        s = (s[1::2] + s[::2])[:h]
    return s


class Chan:
    def __init__(self, nc, name):
        self.h = nc.alloc_semaphore(name=name)
        self.n = 0


class Prog:
    ENGINES = ("tensor", "vector", "scalar", "gpsimd", "sync")

    def __init__(self, nc):
        self.nc = nc
        self.streams = {e: [] for e in self.ENGINES}
        self._chans = {}

    def chan(self, name):
        if name not in self._chans:
            self._chans[name] = Chan(self.nc, name)
        return self._chans[name]

    MAX_SIG = 1
    CHAINED = ("vector", "scalar", "gpsimd")

    def step(self, eng, fn, wait=(), sig=(), isdma=False, dep=True):
        waits = []
        for w in wait:
            ch, v = (w if isinstance(w, tuple) else (w, None))
            v = ch.n if v is None else v
            if v > 0:
                waits.append((ch.h, v))
        sigs = []
        for s in sig:
            ch, inc = (s if isinstance(s, tuple) else (s, 1))
            ch.n += inc
            sigs.append((ch.h, inc))
        drain = (eng in self.CHAINED) and dep and not isdma
        self.streams[eng].append((fn, waits, sigs[:self.MAX_SIG], drain))
        for extra in sigs[self.MAX_SIG:]:
            self.streams[eng].append(("DNOP", [], [extra], False))

    def replay(self, blk):
        def mk(eng):
            def body(e):
                first = True
                for fn, waits, sigs, drain in self.streams[eng]:
                    if fn == "DNOP":
                        e.drain()
                        inst = e.nop()
                        for sem, inc in sigs:
                            inst.then_inc(sem, inc)
                        continue
                    if drain and not first:
                        e.drain()
                    first = False
                    for sem, v in waits:
                        e.wait_ge(sem, v)
                    inst = fn(e) if fn is not None else None
                    for sem, inc in sigs:
                        inst.then_inc(sem, inc)
            return body
        blk.tensor(mk("tensor"))
        blk.vector(mk("vector"))
        blk.scalar(mk("scalar"))
        blk.gpsimd(mk("gpsimd"))
        blk.sync(mk("sync"))

    def counts(self):
        return {e: len(s) for e, s in self.streams.items()}


class Carver:
    def __init__(self, arena_ap, total_bytes):
        self.a = arena_ap
        self.total = total_bytes
        self.off = 0
        self.marks = []
        self.peak = 0

    def take(self, n_elems, dtype):
        esz = 4 if dtype == F32 else 2
        al = (self.off + 63) // 64 * 64
        start = al // 2
        nb = n_elems * esz
        self.off = al + nb
        self.peak = max(self.peak, self.off)
        assert self.off <= self.total, f"SBUF arena overflow: {self.off} > {self.total}"
        sl = self.a[:, start:start + nb // 2]
        if dtype == F32:
            sl = sl.bitcast(F32)
        return sl

    def push(self):
        self.marks.append(self.off)

    def pop(self):
        self.off = self.marks.pop()


def build(cfg: Cfg, upto=99):
    c = cfg
    nc = bass.Bass("TRN2", target_bir_lowering=False, debug=False)

    x_in = nc.dram_tensor("x", [c.TOK, c.D], F32, kind="ExternalInput").ap()
    wqkvT = nc.dram_tensor("wqkvT", [c.D, 3 * c.D], F32, kind="ExternalInput").ap()
    woutT = nc.dram_tensor("woutT", [c.D, c.D], F32, kind="ExternalInput").ap()
    basewT = nc.dram_tensor("basewT", [c.D, c.D], F32, kind="ExternalInput").ap()
    at_in = nc.dram_tensor("AT", [c.D, c.R], F32, kind="ExternalInput").ap()
    rwt_in = nc.dram_tensor("RWT", [c.D, c.E], F32, kind="ExternalInput").ap()
    bstk_in = nc.dram_tensor("Bstk", [c.E * c.R, c.D], F32, kind="ExternalInput").ap()
    baseb_in = nc.dram_tensor("baseb", [1, c.D], F32, kind="ExternalInput").ap()
    qlnw_in = nc.dram_tensor("qlnw", [128, c.DCH], F32, kind="ExternalInput").ap()
    qlnb_in = nc.dram_tensor("qlnb", [128, c.DCH], F32, kind="ExternalInput").ap()
    klnw_in = nc.dram_tensor("klnw", [128, c.DCH], F32, kind="ExternalInput").ap()
    klnb_in = nc.dram_tensor("klnb", [128, c.DCH], F32, kind="ExternalInput").ap()
    slop_in = nc.dram_tensor("slopes", [1, c.HL], F32, kind="ExternalInput").ap()

    y_out = nc.dram_tensor("out", [c.TOK, c.D], F32, kind="ExternalOutput").ap()

    qstage = nc.dram_tensor("qstage", [c.TOK, c.D], BF16, kind="Internal").ap()
    kstage = nc.dram_tensor("kstage", [c.TOK, c.D], BF16, kind="Internal").ap()
    x1d = nc.dram_tensor("x1d", [c.TOK, c.D], F32, kind="Internal").ap()
    combT_d = nc.dram_tensor("combT_d", [c.E, c.TOK], F32, kind="Internal").ap()
    agkq_in = nc.dram_tensor("agkq_in", [2 * c.D, c.TOK], BF16, kind="Internal").ap()
    agk_out = nc.dram_tensor("agk_out", [c.G * c.D, c.TOK], BF16, kind="Internal").ap()
    agq_out = nc.dram_tensor("agq_out", [c.G * c.D, c.TOK], BF16, kind="Internal").ap()
    agv_in = nc.dram_tensor("agv_in", [c.TOK, c.D], BF16, kind="Internal").ap()
    agv_out = nc.dram_tensor("agv_out", [c.S, c.D], BF16, kind="Internal").ap()
    agat_in = nc.dram_tensor("agat_in", [c.D, c.TOK], BF16, kind="Internal").ap()
    agat_out = nc.dram_tensor("agat_out", [c.G * c.D, c.TOK], BF16, kind="Internal").ap()

    RG = [[0, 1, 2, 3], [4, 5, 6, 7]]

    with (
        nc.sbuf_tensor("arena", [128, 105472], BF16) as arena_t,
        nc.psum_tensor("ps_s0", [128, 1024], F32) as ps_s0_t,
        nc.psum_tensor("ps_s1", [128, 1024], F32) as ps_s1_t,
        nc.psum_tensor("ps_o", [128, 512], F32) as ps_o_t,
        nc.psum_tensor("ps_t", [128, 2048], BF16) as ps_t_t,
        nc.psum_tensor("ps_w", [128, 512], F32) as ps_w_t,
        nc.Block() as blk,
    ):
        arena = arena_t.ap()
        ps_s0 = ps_s0_t.ap()
        ps_s1 = ps_s1_t.ap()
        ps_o = ps_o_t.ap()
        ps_t = ps_t_t.ap()
        ps_w = ps_w_t.ap()
        ps_mm = [ps_s0[:, 0:512], ps_s0[:, 512:1024],
                 ps_s1[:, 0:512], ps_s1[:, 512:1024]]

        P = Prog(nc)
        cv = Carver(arena, 156672)
        SCR_BASE = 156672
        scr = Carver(arena, 210944)
        scr.off = SCR_BASE

        id_bf = cv.take(128, BF16)
        id_f32 = cv.take(128, F32)
        ones_f32 = cv.take(128, F32)
        sclrow = cv.take(64, F32)
        onesrow_bf = cv.take(128, BF16)
        r0n = cv.take(c.R0W, F32)
        qlnw = cv.take(c.DCH, F32)
        qlnb = cv.take(c.DCH, F32)
        klnw = cv.take(c.DCH, F32)
        klnb = cv.take(c.DCH, F32)
        slope_col = cv.take(max(c.HL, 8), F32)
        sloprow = cv.take(max(c.HL, 8), F32)
        statw = cv.take(6 * (c.D // 512), F32)
        stat2 = cv.take(8, F32)
        tcol0 = cv.take(16, F32)
        tcol1 = cv.take(8, F32)
        rlrow = cv.take(128, F32)
        rlb = cv.take(128, F32)
        epscol = cv.take(1, F32)
        nhalf = cv.take(1, F32)
        selm = cv.take(128, F32)

        mean_c = stat2[:, 0:1]
        var_c = stat2[:, 1:2]
        rstd_c = stat2[:, 2:3]
        negm_c = stat2[:, 3:4]

        _rank_cache = {}
        _fill_cache = {}

        def fillreg(e, v):
            import struct
            k = (id(e), v)
            if k not in _fill_cache:
                _fill_cache[k] = e.to_reg(
                    struct.unpack('<i', struct.pack('<f', v))[0])
            return _fill_cache[k]

        def rankof(e):
            k = id(e)
            if k not in _rank_cache:
                _rank_cache[k] = e.partition_id() % c.G
            return _rank_cache[k]

        def dma(eng, out, in_, wait=(), sig=()):
            P.step(eng, lambda e, o=out, i=in_: e.dma_start(out=o, in_=i),
                   wait=wait, sig=sig, isdma=True)

        ld = P.chan("ld")
        cst = P.chan("cst")
        dma("sync", qlnw, qlnw_in, sig=[(ld, 16)])
        dma("sync", qlnb, qlnb_in, sig=[(ld, 16)])
        dma("sync", klnw, klnw_in, sig=[(ld, 16)])
        dma("sync", klnb, klnb_in, sig=[(ld, 16)])
        dma("sync", sloprow[0:1, 0:c.HL], slop_in, sig=[(ld, 16)])
        ldg = P.chan("ldg")

        P.step("gpsimd", lambda e: e.memset(id_bf[:, :], 1.0))
        P.step("gpsimd", lambda e: e.affine_select(
            id_bf[:, :], id_bf[:, :], [[1, 128]], ALU.is_equal, 0.0,
            base=0, channel_multiplier=-1), sig=[(cst, 1)])
        P.step("gpsimd", lambda e: e.memset(id_f32[:, :], 1.0))
        P.step("gpsimd", lambda e: e.affine_select(
            id_f32[:, :], id_f32[:, :], [[1, 128]], ALU.is_equal, 0.0,
            base=0, channel_multiplier=-1), sig=[(cst, 1)])
        P.step("gpsimd", lambda e: e.memset(ones_f32[0:1, :], 1.0), sig=[(cst, 1)])
        P.step("gpsimd", lambda e: e.memset(sclrow[0:1, :], c.SCALING), sig=[(cst, 1)])
        P.step("gpsimd", lambda e: e.memset(onesrow_bf[0:1, :], 1.0), sig=[(cst, 1)])
        P.step("gpsimd", lambda e: e.memset(epscol, c.EPS), sig=[(cst, 1)])
        P.step("gpsimd", lambda e: e.memset(nhalf, -0.5), sig=[(cst, 1)])
        P.step("gpsimd", lambda e: e.memset(selm[0:2, :], c.SCALING), sig=[(cst, 1)])
        P.step("gpsimd", lambda e: e.affine_select(
            selm[0:2, :], selm[0:2, :], [[1, 128]], ALU.is_ge,
            fillreg(e, 0.0), base=0, channel_multiplier=-64), sig=[(cst, 1)])
        P.step("gpsimd", lambda e: e.affine_select(
            selm[0:2, :], selm[0:2, :], [[-1, 128]], ALU.is_ge,
            fillreg(e, 0.0), base=63, channel_multiplier=64), sig=[(cst, 1)])
        P.step("gpsimd", lambda e: e.iota(
            r0n[:, :], [[1, c.R0W]], base=-127, channel_multiplier=-1,
            allow_small_or_imprecise_dtypes=True),
            sig=[(cst, 1)])
        P.step("vector", lambda e: e.tensor_scalar_mul(qlnw, qlnw, c.ISQ),
               wait=[ld], sig=[(cst, 1)])
        P.step("vector", lambda e: e.tensor_scalar_mul(qlnb, qlnb, c.ISQ),
               wait=[ld], sig=[(cst, 1)])
        P.step("tensor", lambda e: e.matmul(
            ps_w[:, 0:c.HL], ones_f32[0:1, :], sloprow[0:1, 0:c.HL],
            start=True, stop=True), wait=[ld, cst], sig=[(cst, 1)])
        P.step("vector", lambda e: e.tensor_copy(slope_col[:, 0:c.HL], ps_w[:, 0:c.HL]),
               wait=[cst], sig=[(cst, 1)])
        CST = cst.n
        LDF = ld.n

        tp_state = {"par": 0, "last": [None, None]}

        def tp_half():
            h = tp_state["par"] % 2
            tp_state["par"] += 1
            return h

        def emit_ln_transpose(src_dram, hT, gate):
            htile = scr.take(c.D, BF16)
            xt = [scr.take(c.D, F32), scr.take(c.D, F32)]
            ch_x = [P.chan(f"lt_x{i}") for i in range(2)]
            ch_xf = [P.chan(f"lt_xf{i}") for i in range(2)]
            ch_st = P.chan("lt_st")
            ch_h = P.chan("lt_h")
            ch_tp = P.chan("lt_tp")
            ch_ev = P.chan("lt_ev")
            ngrp = c.DCH // 8
            for tt in range(c.TT):
                sl = tt % 2
                src = src_dram[tt * 128:(tt + 1) * 128, :]
                w = list(gate) + ([ch_xf[sl]] if ch_xf[sl].n > 0 else [])
                dma("sync", xt[sl], src, wait=w, sig=[(ch_x[sl], 16)])
                nseg = c.D // 512
                for sgi in range(nseg):
                    P.step("vector", lambda e, s=sgi, xi=xt[sl]: e.bn_stats(
                        statw[:, 6 * s:6 * s + 6], xi[:, 512 * s:512 * (s + 1)]),
                        wait=[ch_x[sl]] if sgi == 0 else (), dep=(sgi == 0))
                P.step("vector", lambda e, n=nseg: e.bn_aggr(
                    stat2[:, 0:2], statw[:, 0:6 * n]))
                P.step("vector", lambda e: e.tensor_scalar_mul(negm_c, mean_c, -1.0),
                       sig=[(ch_st, 1)])
                P.step("scalar", lambda e: e.activation(
                    rstd_c, var_c, AF.Ln, bias=epscol), wait=[ch_st])
                P.step("scalar", lambda e: e.activation(
                    rstd_c, rstd_c, AF.Exp, scale=nhalf), sig=[(ch_st, 1)])
                P.step("vector", lambda e, xi=xt[sl]: e.tensor_scalar(
                    htile, xi, negm_c, rstd_c, op0=ALU.add, op1=ALU.mult),
                    wait=[ch_st], sig=[(ch_h, 1), (ch_xf[sl], 1)])
                for g in range(ngrp):
                    hf = tp_half()
                    war = tp_state["last"][hf]
                    for j in range(8):
                        cc = g * 8 + j
                        P.step("tensor", lambda e, ci=cc, j2=j, h2=hf: e.transpose(
                            ps_t[:, h2 * 1024 + 128 * j2:h2 * 1024 + 128 * (j2 + 1)],
                            htile[:, 128 * ci:128 * (ci + 1)], id_bf[:, :]),
                            wait=([ch_h, (cst, CST)] + ([war] if war else [])) if j == 0 else (),
                            sig=[(ch_tp, 1)] if j == 7 else ())
                    dst = hT.rearrange("p (ch t) -> p ch t", t=c.TOK)
                    P.step("vector", lambda e, g2=g, h2=hf, d=dst, t0=tt: e.tensor_copy(
                        d[:, g2 * 8:g2 * 8 + 8, t0 * 128:(t0 + 1) * 128],
                        ps_t[:, h2 * 1024:(h2 + 1) * 1024].rearrange(
                            "p (b t) -> p b t", t=128)),
                        wait=[ch_tp], sig=[(ch_ev, 1)], dep=False)
                    tp_state["last"][hf] = (ch_ev, ch_ev.n)
            return ch_ev

        def emit_qk_ln(stage_dram, wcol, bcol_g, row0, gate):
            ktile = scr.take(c.D, BF16)
            ntile = scr.take(c.D, BF16)
            otile = scr.take(c.DCH * 128, BF16)
            ch_x = P.chan("qk_x")
            ch_st = P.chan("qk_st")
            ch_n = P.chan("qk_n")
            ch_tp = P.chan("qk_tp")
            ch_ev = P.chan("qk_ev")
            ch_sd = P.chan("qk_sd")
            ngrp = c.DCH // 8
            for tt in range(c.TT):
                src = stage_dram[tt * 128:(tt + 1) * 128, :]
                dma("sync", ktile, src,
                    wait=list(gate) + ([ch_n] if ch_n.n > 0 else []),
                    sig=[(ch_x, 16)])
                nseg = c.D // 512
                for sgi in range(nseg):
                    P.step("vector", lambda e, s=sgi: e.bn_stats(
                        statw[:, 6 * s:6 * s + 6], ktile[:, 512 * s:512 * (s + 1)]),
                        wait=[ch_x] if sgi == 0 else (), dep=(sgi == 0))
                P.step("vector", lambda e, n=nseg: e.bn_aggr(
                    stat2[:, 0:2], statw[:, 0:6 * n]))
                P.step("vector", lambda e: e.tensor_scalar_mul(negm_c, mean_c, -1.0),
                       sig=[(ch_st, 1)])
                P.step("scalar", lambda e: e.activation(
                    rstd_c, var_c, AF.Ln, bias=epscol), wait=[ch_st])
                P.step("scalar", lambda e: e.activation(
                    rstd_c, rstd_c, AF.Exp, scale=nhalf), sig=[(ch_st, 1)])
                P.step("vector", lambda e: e.tensor_scalar(
                    ntile, ktile, negm_c, rstd_c, op0=ALU.add, op1=ALU.mult),
                    wait=[ch_st, (ch_tp, ch_tp.n)], sig=[(ch_n, 1)])
                for g in range(ngrp):
                    hf = tp_half()
                    war = tp_state["last"][hf]
                    for j in range(8):
                        cc = g * 8 + j
                        P.step("tensor", lambda e, ci=cc, j2=j, h2=hf: e.transpose(
                            ps_t[:, h2 * 1024 + 128 * j2:h2 * 1024 + 128 * (j2 + 1)],
                            ntile[:, 128 * ci:128 * (ci + 1)], id_bf[:, :]),
                            wait=([ch_n] + ([war] if war else [])) if j == 0 else (),
                            sig=[(ch_tp, 1)] if j == 7 else ())
                    for j in range(8):
                        cc = g * 8 + j
                        P.step("vector", lambda e, ci=cc, j2=j, h2=hf: e.tensor_scalar(
                            otile[:, 128 * ci:128 * (ci + 1)],
                            ps_t[:, h2 * 1024 + 128 * j2:h2 * 1024 + 128 * (j2 + 1)],
                            wcol[:, ci:ci + 1], bcol_g[:, ci:ci + 1],
                            op0=ALU.mult, op1=ALU.add),
                            wait=([ch_tp] + ([(ch_sd, ch_sd.n)] if g == 0 else []))
                                 if j == 0 else (),
                            sig=[(ch_ev, 1)] if j == 7 else (), dep=False)
                    tp_state["last"][hf] = (ch_ev, ch_ev.n)
                dstv = agkq_in.rearrange("(r p) t -> p r t", p=128)
                P.step("sync", lambda e, t0=tt, d=dstv: e.dma_start(
                    out=d[:, row0 // 128:row0 // 128 + c.DCH,
                          t0 * 128:(t0 + 1) * 128],
                    in_=otile.rearrange("p (r t) -> p r t", t=128)),
                    wait=[ch_ev], sig=[(ch_sd, 16)])
            return ch_sd

        cv.push()
        hT = cv.take(c.DCH * c.TOK, BF16)
        scr.off = SCR_BASE
        ch_h1 = emit_ln_transpose(x_in, hT, gate=[(cst, CST), (ld, LDF)])
        H1EV = ch_h1.n


        if upto < 2:
            P.replay(blk)
            return nc, P
        wslab = cv.take(2 * c.DCH * 512, BF16)
        stg = cv.take(2 * c.TT * 512, BF16)
        ch_wl = [P.chan(f"wl{i}") for i in range(2)]
        ch_wf = [P.chan(f"wf{i}") for i in range(2)]
        ch_ps = [P.chan(f"ps{i}") for i in range(4)]
        ch_pf = [P.chan(f"pf{i}") for i in range(4)]
        ch_sg = [P.chan(f"sg{i}") for i in range(2)]
        ch_sdq = P.chan("sdq")
        ch_sdk = P.chan("sdk")
        ch_sdv = P.chan("sdv")
        stage_last = [None, None]

        NEB = 3 * c.EB1
        cc_k = P.chan("cc_k")
        cc_q = P.chan("cc_q")
        cc_v = P.chan("cc_v")
        wsrc = wqkvT.rearrange("(ch p) e -> p ch e", p=128)
        ev_per_tile = c.DCH // 8

        def wslab_half(slot):
            return wslab[:, slot * c.DCH * 512:(slot + 1) * c.DCH * 512]

        def load_slab(src_view, col0, target, wait, sigch):
            segs = max(1, c.DCH // 8)
            per = c.DCH // segs
            for sgi in range(segs):
                P.step("gpsimd", lambda e, s=sgi, sv=src_view, t=target, c0=col0, p=per: e.dma_start(
                    out=t.rearrange("p (ch k) -> p ch k", k=512)[:, s * p:(s + 1) * p, :],
                    in_=sv[:, s * p:(s + 1) * p, c0:c0 + 512]),
                    wait=wait if sgi == 0 else (), sig=[(sigch, 16)], isdma=True)
            return 16 * segs

        eb_order = (list(range(c.EB1, 2 * c.EB1)) +
                    list(range(2 * c.EB1, 3 * c.EB1)) +
                    list(range(0, c.EB1)))
        for idx, eb in enumerate(eb_order):
            sl = idx % 2
            w = [ch_wf[sl]] if idx >= 2 else []
            load_slab(wsrc, eb * 512, wslab_half(sl), w, ch_wl[sl])
            for tt in range(c.TT):
                pslot = tt % 4
                for ci in range(c.DCH):
                    wl = []
                    if ci == 0:
                        wl = [ch_wl[sl], (ch_h1, (tt + 1) * ev_per_tile), ch_pf[pslot]]
                    sg = []
                    if ci == c.DCH - 1:
                        sg = [(ch_ps[pslot], 1)]
                        if tt == c.TT - 1:
                            sg.append((ch_wf[sl], 1))
                    P.step("tensor", lambda e, p2=pslot, s=sl, c2=ci, t2=tt: e.matmul(
                        ps_mm[p2],
                        hT.rearrange("p (ch t) -> p ch t", t=c.TOK)[:, c2,
                                     t2 * 128:(t2 + 1) * 128],
                        wslab_half(s).rearrange("p (ch k) -> p ch k", k=512)[:, c2, :],
                        start=(c2 == 0), stop=(c2 == c.DCH - 1)),
                        wait=wl, sig=sg)
                P.step("vector", lambda e, p2=pslot, s=sl, t2=tt: e.tensor_scalar(
                    stg[:, (s * c.TT + t2) * 512:(s * c.TT + t2 + 1) * 512],
                    ps_mm[p2], -c.CLIP, c.CLIP, op0=ALU.max, op1=ALU.min),
                    wait=[ch_ps[pslot]] + ([stage_last[sl]] if (tt == 0 and stage_last[sl] is not None) else []),
                    sig=[(ch_pf[pslot], 1), (ch_sg[sl], 1)], dep=False)
            sec = eb // c.EB1
            ebr = eb % c.EB1
            stg_v = stg[:, sl * c.TT * 512:(sl + 1) * c.TT * 512].rearrange(
                "p (t k) -> p t k", k=512)
            if sec == 0:
                dst = qstage.rearrange("(t p) e -> p t e", p=128)[:, :, ebr * 512:(ebr + 1) * 512]
            elif sec == 1:
                dst = kstage.rearrange("(t p) e -> p t e", p=128)[:, :, ebr * 512:(ebr + 1) * 512]
            else:
                dst = agv_in.rearrange("(t p) e -> p t e", p=128)[:, :, ebr * 512:(ebr + 1) * 512]
            ch_sec = ch_sdq if sec == 0 else (ch_sdk if sec == 1 else ch_sdv)
            P.step("sync", lambda e, d=dst, sv=stg_v: e.dma_start(out=d, in_=sv),
                   wait=[ch_sg[sl]], sig=[(ch_sec, 16)])
            stage_last[sl] = ch_sec
            if idx == c.EB1 - 1:
                scr.off = SCR_BASE
                ch_k_sd = emit_qk_ln(kstage, klnw, klnb, 0, [ch_sdk])
                P.step("gpsimd", lambda e: e.collective_compute(
                    "AllGather", ALU.bypass, replica_groups=RG,
                    ins=[agkq_in[0:c.D, :]], outs=[agk_out[:, :]]),
                    wait=[ch_k_sd], sig=[(cc_k, 1)], isdma=True)
            if idx == 2 * c.EB1 - 1:
                P.step("gpsimd", lambda e: e.collective_compute(
                    "AllGather", ALU.bypass, replica_groups=RG,
                    ins=[agv_in[:, :]], outs=[agv_out[:, :]]),
                    wait=[ch_sdv], sig=[(cc_v, 1)], isdma=True)


        if upto < 3:
            P.replay(blk)
            return nc, P
        ch_q_sd = emit_qk_ln(qstage, qlnw, qlnb, c.D, [ch_sdq])
        P.step("gpsimd", lambda e: e.collective_compute(
            "AllGather", ALU.bypass, replica_groups=RG,
            ins=[agkq_in[c.D:2 * c.D, :]], outs=[agq_out[:, :]]),
            wait=[ch_q_sd], sig=[(cc_q, 1)], isdma=True)


        if upto < 5:
            P.replay(blk)
            return nc, P
        cv.pop()
        cv.push()
        VS = (c.S // 128) * 256
        pairKQ = cv.take(2 * 4 * c.S, BF16)
        pairV = cv.take(2 * VS, BF16)
        sbuf_Sb2 = [cv.take(c.S, F32), cv.take(c.S, F32)]
        pbuf = cv.take(c.KCH, BF16)
        ptbuf = cv.take(2 * c.KCH, BF16)
        astage2 = [cv.take(128, BF16), cv.take(128, BF16)]
        rlrow2 = [rlrow[0:1, 0:128], cv.take(128, F32)[0:1, 0:128]]
        rlb2 = [rlb, cv.take(128, F32)]
        mstage2 = [cv.take(4, F32), cv.take(4, F32)]
        accst2 = [cv.take(4, F32), cv.take(4, F32)]

        lcol2 = [tcol0[:, 0:1], tcol0[:, 8:9]]
        mrun2 = [tcol0[:, 1:2], tcol0[:, 9:10]]
        bcol2 = [tcol0[:, 5:6], tcol0[:, 13:14]]
        rlc2 = [tcol0[:, 6:7], tcol0[:, 14:15]]

        scr.off = SCR_BASE
        r0full = scr.take(c.S, F32)
        ch_r0 = P.chan("p5_r0")
        P.step("gpsimd", lambda e: e.iota(
            r0full[:, :], [[1, c.S]], base=-(c.S - 128), channel_multiplier=-1,
            allow_small_or_imprecise_dtypes=True),
            wait=[cc_k, cc_q, cc_v], sig=[(ch_r0, 1)], dep=False)

        ch_pl = [P.chan(f"p5_pl{i}") for i in range(2)]
        ch_pfree = [P.chan(f"p5_pfree{i}") for i in range(2)]
        ch_pes = [P.chan(f"p5_pes{i}") for i in range(2)]
        ch_dsb = [P.chan(f"p5_dsb{i}") for i in range(2)]
        ch_gmx = P.chan("p5_gmx")
        ch_msk = P.chan("p5_msk")
        ch_exp = P.chan("p5_exp")
        ch_ptp = P.chan("p5_ptp")
        ch_pte = P.chan("p5_pte")
        ch_pvg = P.chan("p5_pvg")
        ch_oev = P.chan("p5_oev")
        ch_pvd = P.chan("p5_pvd")
        ch_fin = P.chan("p5_fin")
        ch_adr_g = [P.chan(f"p5_adr{g}") for g in range(c.QT // c.TT)]
        adr_hist = []
        cc_at = P.chan("cc_at")

        gk = agk_out.rearrange("(i rb hl p) t -> i rb hl p t", rb=c.G, hl=c.HL, p=128)
        gq = agq_out.rearrange("(i rb hl p) t -> i rb hl p t", rb=c.G, hl=c.HL, p=128)
        gv = agv_out.rearrange("(kt p) (rb gp q) -> p kt rb gp q", p=128, rb=c.G, q=256)
        pp_glob = [0]
        exp_qt_end = [0, 0]
        gmx_prev = [None, None]
        ladtag = [None, None]
        rlwar = [None, None]

        def kq_col(slot, hh, kq, k):
            return (slot * 4 * c.S + hh * 2 * c.S +
                    (k // c.TOK) * 2 * c.TOK + kq * c.TOK + (k % c.TOK))

        def kq_sl(slot, hh, kq, k, w):
            b = kq_col(slot, hh, kq, k)
            if (k % c.TOK) + w <= c.TOK:
                return pairKQ[:, b:b + w]
            assert k % c.TOK == 0 and w % c.TOK == 0
            nb = w // c.TOK
            return pairKQ[:, b:b + (nb - 1) * 2 * c.TOK + c.TOK].rearrange(
                "p (b2 t) -> p b2 t", t=c.TOK)[:, ::2, :] if False else bass.AP(
                tensor=pairKQ.tensor, offset=pairKQ.offset + b,
                ap=[list(pairKQ.ap[0]), [2 * c.TOK, nb], [1, c.TOK]])
        ptlast = [None, None]

        pend5 = [None]

        def emit_passA(ctx):
            sl, hh, qt, qp = ctx["sl"], ctx["hh"], ctx["qt"], ctx["qp"]
            scol = ctx["scol"]
            Sb = sbuf_Sb2[qp]
            mst = mstage2[qp]
            klen = 128 * (qt + 1)
            nch = (klen + c.KCH - 1) // c.KCH
            ctx["nch"] = nch
            ctx["nkt"] = klen // 128
            ctx["klen"] = klen
            atags = []
            for chi in range(nch):
                k0 = chi * c.KCH
                w = min(c.KCH, klen - k0)
                pp = pp_glob[0] % 2
                pp_glob[0] += 1
                ctx.setdefault("pps", []).append(pp)
                pss = ps_s0 if pp == 0 else ps_s1
                nmm = (w + 511) // 512
                for j in range(nmm):
                    wj = min(512, w - j * 512)
                    wl = []
                    if j == 0:
                        wl = [ch_dsb[pp]]
                        if ctx["first"]:
                            wl.append(ch_pl[sl])
                    qsl = kq_sl(sl, hh, 1, qt * 128, 128)
                    ksl = kq_sl(sl, hh, 0, k0 + j * 512, wj)
                    P.step("tensor", lambda e, j2=j, w2=wj,
                           ps=pss, kh=ksl, qh=qsl: e.matmul(
                        ps[:, j2 * 512:j2 * 512 + w2],
                        qh, kh, start=True, stop=True),
                        wait=wl,
                        sig=[(ch_pes[pp], 1)] if j == nmm - 1 else ())
                c0 = c.S - 128 * (qt + 1) + k0
                wl = [ch_pes[pp], (ch_exp, exp_qt_end[qp])]
                if ctx["first"] and chi == 0 and sl == 0 and ch_r0.n == 1:
                    wl.append((ch_r0, 1))
                P.step("vector", lambda e, w2=w, k2=k0, c2=c0, ps=pss, sc=scol,
                       sb=Sb: e.scalar_tensor_tensor(
                    sb[:, k2:k2 + w2], r0full[:, c2:c2 + w2], sc,
                    ps[:, 0:w2], op0=ALU.mult, op1=ALU.add),
                    wait=wl, sig=[(ch_dsb[pp], 1)], dep=False)
                atags.append((ch_dsb[pp], ch_dsb[pp].n))
            P.step("gpsimd", lambda e, kl=klen, sb=Sb: e.affine_select(
                sb[:, kl - 128:kl], sb[:, kl - 128:kl],
                [[-1, 128]], ALU.is_ge, fillreg(e, NEG_BIG),
                base=0, channel_multiplier=1),
                wait=[atags[-1]], sig=[(ch_msk, 1)], dep=False)
            msk_tag = (ch_msk, ch_msk.n)
            for chi in range(nch):
                k0 = chi * c.KCH
                w = min(c.KCH, klen - k0)
                wl = [atags[chi]]
                if chi == nch - 1:
                    wl.append(msk_tag)
                if chi == 0 and gmx_prev[qp] is not None:
                    wl.append(gmx_prev[qp])
                P.step("vector", lambda e, w2=w, k2=k0, sb=Sb, ci=chi, m=mst:
                       e.reduce_max(m[:, ci:ci + 1], sb[:, k2:k2 + w2], axis=AX.X),
                       wait=wl, dep=False)
            if nch > 1:
                P.step("vector", lambda e, m=mst, n2=nch, mr=mrun2[qp]:
                       e.reduce_max(mr, m[:, 0:n2], axis=AX.X))
                msrc = mrun2[qp]
            else:
                msrc = mst[:, 0:1]
            P.step("vector", lambda e, b=bcol2[qp], m=msrc: e.tensor_scalar_mul(
                b, m, -1.0),
                wait=[(ch_exp, exp_qt_end[qp])], sig=[(ch_gmx, 1)])
            gmx_prev[qp] = (ch_gmx, ch_gmx.n)
            ctx["gmx_end"] = ch_gmx.n

        def emit_passB(ctx):
            sl, hh, qt, qp = ctx["sl"], ctx["hh"], ctx["qt"], ctx["qp"]
            hloc = ctx["hloc"]
            nch, nkt, klen = ctx["nch"], ctx["nkt"], ctx["klen"]
            Sb = sbuf_Sb2[qp]
            acc = accst2[qp]
            ktg = 0
            expt = []
            for chi in range(nch):
                k0 = chi * c.KCH
                w = min(c.KCH, klen - k0)
                nkt_ch = w // 128
                pslot = chi % 2
                wl = [(ch_gmx, ctx["gmx_end"]), (ch_ptp, ch_ptp.n)]
                if chi == 0 and ladtag[qp] is not None:
                    wl.append(ladtag[qp])
                P.step("scalar", lambda e, w2=w, k2=k0, sb=Sb, ci=chi, a=acc,
                       b=bcol2[qp]: e.activation(
                    pbuf[:, 0:w2], sb[:, k2:k2 + w2], AF.Exp,
                    bias=b, accum_out=a[:, ci:ci + 1]),
                    wait=wl, sig=[(ch_exp, 1)], dep=False)
                expt.append((ch_exp, ch_exp.n))
                hf = tp_half()
                war = tp_state["last"][hf]
                for j in range(nkt_ch):
                    P.step("tensor", lambda e, k2=j, h2=hf: e.transpose(
                        ps_t[:, h2 * 1024 + k2 * 128:h2 * 1024 + (k2 + 1) * 128],
                        pbuf[:, k2 * 128:(k2 + 1) * 128], id_bf[:, :]),
                        wait=([expt[-1]] + ([war] if war else [])) if j == 0 else (),
                        sig=[(ch_ptp, 1)] if j == nkt_ch - 1 else ())
                P.step("scalar", lambda e, h2=hf, n2=nkt_ch, p2=pslot: e.activation(
                    ptbuf[:, p2 * c.KCH:p2 * c.KCH + n2 * 128],
                    ps_t[:, h2 * 1024:h2 * 1024 + n2 * 128], AF.Copy),
                    wait=[(ch_ptp, ch_ptp.n)] + ([ptlast[pslot]] if ptlast[pslot] else []),
                    sig=[(ch_pte, 1)], dep=False)
                tp_state["last"][hf] = (ch_pte, ch_pte.n)
                for j in range(nkt_ch):
                    kg = ktg + j
                    wl = [ch_pte] if j == 0 else []
                    if kg == 0:
                        wl.append(ch_oev)
                    sg = []
                    if j == nkt_ch - 1:
                        sg.append((ch_pvg, 1))
                    if kg == nkt - 1:
                        sg.append((ch_pvd, 1))
                        if qt == c.QT - 1 and hh == 1:
                            sg.append((ch_pfree[sl], 1))
                    P.step("tensor", lambda e, kg2=kg, j2=j, s=sl, h3=hh, p2=pslot, nk=nkt: e.matmul(
                        ps_o[:, 0:128],
                        pairV[:, s * VS:(s + 1) * VS].rearrange(
                            "p (kt q) -> p kt q", q=256)[:, kg2,
                            h3 * 128:(h3 + 1) * 128],
                        ptbuf[:, p2 * c.KCH + j2 * 128:p2 * c.KCH + (j2 + 1) * 128],
                        start=(kg2 == 0), stop=(kg2 == nk - 1)),
                        wait=wl, sig=sg)
                ptlast[pslot] = (ch_pvg, ch_pvg.n)
                ktg += nkt_ch
            exp_qt_end[qp] = ch_exp.n
            if nch > 1:
                wl = [expt[-1]] + ([rlwar[qp]] if rlwar[qp] else [])
                P.step("vector", lambda e, l=lcol2[qp], a=acc, n2=nch: e.reduce_sum(
                    l, a[:, 0:n2], axis=AX.X), wait=wl, dep=False)
                P.step("vector", lambda e, r=rlc2[qp], l=lcol2[qp]: e.reciprocal(
                    r, l), sig=[(ch_fin, 1)])
            else:
                wl = [expt[-1]] + ([rlwar[qp]] if rlwar[qp] else [])
                P.step("vector", lambda e, r=rlc2[qp], a=acc: e.reciprocal(
                    r, a[:, 0:1]), wait=wl, sig=[(ch_fin, 1)], dep=False)
            ladtag[qp] = (ch_fin, ch_fin.n)
            P.step("tensor", lambda e, r=rlc2[qp]: e.transpose(
                ps_w[0:1, 0:128], r, id_f32[:, :]),
                wait=[ch_fin], sig=[(ch_fin, 1)])
            P.step("vector", lambda e, rr=rlrow2[qp]: e.tensor_copy(
                rr, ps_w[0:1, 0:128]),
                wait=[ch_fin], sig=[(ch_fin, 1)], dep=False)
            P.step("tensor", lambda e, rr=rlrow2[qp]: e.matmul(
                ps_w[:, 128:256], ones_f32[0:1, :], rr,
                start=True, stop=True), wait=[ch_fin], sig=[(ch_fin, 1)])
            P.step("vector", lambda e, rb=rlb2[qp]: e.tensor_copy(
                rb, ps_w[:, 128:256]),
                wait=[ch_fin], sig=[(ch_fin, 1)], dep=False)
            rlwar[qp] = (ch_fin, ch_fin.n)
            wl = [ch_fin, ch_pvd]
            seen = {}
            for gg, cnt in adr_hist[-2:]:
                seen[gg] = max(seen.get(gg, 0), cnt)
            for gg, cnt in seen.items():
                wl.append((ch_adr_g[gg], cnt))
            P.step("vector", lambda e, a=astage2[qp], rb=rlb2[qp]: e.tensor_tensor(
                a, ps_o[:, 0:128], rb, ALU.mult),
                wait=wl, sig=[(ch_oev, 1)], dep=False)
            g = qt // c.TT
            adst = agat_in.rearrange("(j h p) t -> p j h t", p=128, h=c.HL)
            P.step("sync", lambda e, q2=qt, h2=hloc, d=adst, a=astage2[qp]: e.dma_start(
                out=d[:, q2 // c.TT, h2, (q2 % c.TT) * 128:(q2 % c.TT + 1) * 128],
                in_=a),
                wait=[ch_oev], sig=[(ch_adr_g[g], 16)], isdma=True)
            adr_hist.append((g, ch_adr_g[g].n))
            if ctx["pr"] == npair - 1 and hh == 1 and (qt + 1) % c.TT == 0:
                rowsz = c.HL * 128
                P.step("gpsimd", lambda e, g2=g, r=rowsz: e.collective_compute(
                    "AllGather", ALU.bypass, replica_groups=RG,
                    ins=[agat_in[g2 * r:(g2 + 1) * r, :]],
                    outs=[agat_out[g2 * c.G * r:(g2 + 1) * c.G * r, :]]),
                    wait=[(ch_adr_g[g], ch_adr_g[g].n)],
                    sig=[(cc_at, 1)], isdma=True)

        npair = c.HL // 2
        for pr in range(npair):
            sl = pr % 2
            wl0 = [cc_k, cc_q, cc_v] if pr == 0 else [ch_pfree[sl]]
            for hh in range(2):
                for kq, gsrc in ((0, gk), (1, gq)):
                    o3 = pairKQ[:, sl * 4 * c.S + hh * 2 * c.S:
                                sl * 4 * c.S + (hh + 1) * 2 * c.S].rearrange(
                        "p (i kq t) -> p kq i t", kq=2, t=c.TOK)[:, kq]
                    P.step("gpsimd", lambda e, p2=pr, h2=hh, o=o3, g3=gsrc: e.dma_start(
                        out=o,
                        in_=g3[:, rankof(e), p2 * 2 + h2].transpose([1, 0, 2])),
                        wait=wl0 if (hh == 0 and kq == 0) else (),
                        sig=[(ch_pl[sl], 16)], isdma=True)
            P.step("gpsimd", lambda e, s=sl, p2=pr: e.dma_start(
                out=pairV[:, s * VS:(s + 1) * VS].rearrange("p (kt q) -> p kt q", q=256),
                in_=gv[:, :, rankof(e), p2, :]),
                sig=[(ch_pl[sl], 16)], isdma=True)

            for hh in range(2):
                hloc = pr * 2 + hh
                for qt in range(c.QT):
                    cur = dict(sl=sl, hh=hh, hloc=hloc, qt=qt, qp=qt % 2, pr=pr,
                               scol=slope_col[:, hloc:hloc + 1],
                               first=(qt == 0 and hh == 0))
                    emit_passA(cur)
                    if pend5[0] is not None:
                        emit_passB(pend5[0])
                    pend5[0] = cur
            if pend5[0] is not None:
                emit_passB(pend5[0])
                pend5[0] = None

        PFREE = [ch_pfree[0].n, ch_pfree[1].n]
        EXPF = ch_exp.n
        cv.pop()


        if upto < 6:
            P.replay(blk)
            return nc, P
        cv.push()
        attnT = cv.take(c.DCH * c.TOK, BF16)
        wslab6 = cv.take(2 * c.DCH * 512, BF16)
        xtl = [cv.take(512, F32), cv.take(512, F32)]
        x1s = [cv.take(512, F32), cv.take(512, F32)]

        ch_atl = P.chan("p6_atl")
        ch_xl = [P.chan(f"xl{i}") for i in range(2)]
        ch_xf = [P.chan(f"xf{i}") for i in range(2)]
        ch_ev = [P.chan(f"ev{i}") for i in range(2)]
        ch_x1sd = [P.chan("p6_x1sd0"), P.chan("p6_x1sd1")]

        gat = agat_out.rearrange("(j i h p) t -> i j h p t", j=c.G, h=c.HL, p=128)
        for i in range(c.G):
            P.step("gpsimd", lambda e, i2=i: e.dma_start(
                out=attnT.rearrange("p (i h t) -> p i h t", i=c.G, h=c.HL)[:, i2],
                in_=gat[i2, rankof(e)].transpose([1, 0, 2])),
                wait=[cc_at] if i == 0 else (), sig=[(ch_atl, 16)], isdma=True)
        ATL = ch_atl.n

        wsrc6 = woutT.rearrange("(ch p) e -> p ch e", p=128)

        def wslab6_half(slot):
            return wslab6[:, slot * c.DCH * 512:(slot + 1) * c.DCH * 512]

        for eb in range(c.EB1):
            sl = eb % 2
            w = [ch_wf[sl]] + ([(ch_pfree[0], PFREE[0]),
                                (ch_pfree[1], PFREE[1])] if eb < 2 else [])
            segs = max(1, c.DCH // 8)
            per = c.DCH // segs
            for sgi in range(segs):
                P.step("gpsimd", lambda e, s=sgi, c0=eb * 512, t=wslab6_half(sl), p=per: e.dma_start(
                    out=t.rearrange("p (ch k) -> p ch k", k=512)[:, s * p:(s + 1) * p, :],
                    in_=wsrc6[:, s * p:(s + 1) * p, c0:c0 + 512]),
                    wait=w if sgi == 0 else (), sig=[(ch_wl[sl], 16)], isdma=True)
            for tt in range(c.TT):
                pslot = tt % 4
                xsl = tt % 2
                dma("sync", xtl[xsl], x_in[tt * 128:(tt + 1) * 128, eb * 512:(eb + 1) * 512],
                    wait=([ch_xf[xsl]] if ch_xf[xsl].n > 0 else []) +
                         ([(ch_exp, EXPF)] if eb == 0 and tt < 2 else []),
                    sig=[(ch_xl[xsl], 16)])
                for ci in range(c.DCH):
                    wl = []
                    if ci == 0:
                        wl = [ch_wl[sl], (ch_atl, ATL), ch_pf[pslot]]
                    sg = []
                    if ci == c.DCH - 1:
                        sg = [(ch_ps[pslot], 1)]
                        if tt == c.TT - 1:
                            sg.append((ch_wf[sl], 1))
                    P.step("tensor", lambda e, p2=pslot, s=sl, c2=ci, t2=tt: e.matmul(
                        ps_mm[p2],
                        attnT.rearrange("p (ch t) -> p ch t", t=c.TOK)[:, c2,
                                        t2 * 128:(t2 + 1) * 128],
                        wslab6_half(s).rearrange("p (ch k) -> p ch k", k=512)[:, c2, :],
                        start=(c2 == 0), stop=(c2 == c.DCH - 1)),
                        wait=wl, sig=sg)
                P.step("vector", lambda e, p2=pslot, xs=xsl: e.tensor_tensor(
                    x1s[xs], ps_mm[p2], xtl[xs], ALU.add),
                    wait=[ch_ps[pslot], ch_xl[xsl], (ch_x1sd[xsl], ch_x1sd[xsl].n)],
                    sig=[(ch_pf[pslot], 1), (ch_xf[xsl], 1), (ch_ev[xsl], 1)],
                    dep=False)
                P.step("sync", lambda e, t2=tt, e2=eb, xs=xsl: e.dma_start(
                    out=x1d[t2 * 128:(t2 + 1) * 128, e2 * 512:(e2 + 1) * 512],
                    in_=x1s[xs]),
                    wait=[ch_ev[xsl]], sig=[(ch_x1sd[xsl], 16)])
        X1SD = [ch_x1sd[0].n, ch_x1sd[1].n]
        cv.pop()


        if upto < 7:
            P.replay(blk)
            return nc, P
        cv.push()
        h2T = cv.take(c.DCH * c.TOK, BF16)
        scr.off = SCR_BASE
        ch_h2 = emit_ln_transpose(x1d, h2T, gate=[(ch_x1sd[0], X1SD[0]),
                                                   (ch_x1sd[1], X1SD[1])])
        H2EV = ch_h2.n
        LTP = P.chan("lt_tp").n


        if upto < 8:
            P.replay(blk)
            return nc, P
        dslab8 = cv.take(2 * c.RC * 512, BF16)
        bslab8 = cv.take(2 * c.DCH * 512, BF16)
        scr.off = SCR_BASE
        basebrow = scr.take(c.D, BF16)
        atslab = scr.take(c.DCH * 64, BF16)
        rwslab = scr.take(c.DCH * 16, BF16)
        resTd = scr.take(c.TOK, F32)
        sstack = scr.take(c.RC * c.TOK, BF16)
        combT = scr.take(128, F32)
        crow = [scr.take(c.TOK, F32), scr.take(c.TOK, F32)]
        lgt = scr.take(16, F32)
        prb = scr.take(16, F32)
        xx2 = scr.take(16, F32)
        eq1 = scr.take(16, F32)
        eq2 = scr.take(16, F32)
        comb = scr.take(16, F32)
        xtl8 = [scr.take(512, F32), scr.take(512, F32)]
        outs8 = [scr.take(512, F32), scr.take(512, F32)]
        rcol = tcol0[:, 8:9]
        m1c = tcol0[:, 9:10]
        m2c = tcol0[:, 10:11]
        denc = tcol0[:, 11:12]
        w1c = tcol0[:, 12:13]
        w2c = tcol0[:, 13:14]
        nm1c = tcol0[:, 14:15]
        sumc = tcol0[:, 15:16]

        ch_cl8 = P.chan("p8_cl")
        ch_lg = P.chan("p8_lg")
        ch_rt = P.chan("p8_rt")
        ch_cmb = P.chan("p8_cmb")
        ch_res = P.chan("p8_res")
        ch_sst = P.chan("p8_sst")
        ch_out = [P.chan("out_dma0"), P.chan("out_dma1")]

        atsrc = at_in.rearrange("(ch p) r -> p ch r", p=128)
        rwsrc = rwt_in.rearrange("(ch p) r -> p ch r", p=128)
        ch_ltp = P.chan("lt_tp")
        P.step("gpsimd", lambda e: e.dma_start(
            out=atslab.rearrange("p (ch r) -> p ch r", r=64), in_=atsrc),
            wait=[(ch_ltp, LTP)], sig=[(ch_cl8, 16)], isdma=True)
        P.step("gpsimd", lambda e: e.dma_start(
            out=rwslab.rearrange("p (ch r) -> p ch r", r=16), in_=rwsrc),
            sig=[(ch_cl8, 16)], isdma=True)
        dma("gpsimd", basebrow[0:1, :], baseb_in, sig=[(ldg, 16)])

        for tt in range(c.TT):
            for ci in range(c.DCH):
                P.step("tensor", lambda e, c2=ci, t2=tt: e.matmul(
                    ps_w[:, 0:16],
                    h2T.rearrange("p (ch t) -> p ch t", t=c.TOK)[:, c2,
                                  t2 * 128:(t2 + 1) * 128],
                    rwslab.rearrange("p (ch r) -> p ch r", r=16)[:, c2, :],
                    start=(c2 == 0), stop=(c2 == c.DCH - 1)),
                    wait=[(ch_cl8, 32), (ch_h2, H2EV), (ch_rt, ch_rt.n)] if ci == 0 else (),
                    sig=[(ch_lg, 1)] if ci == c.DCH - 1 else ())
            P.step("vector", lambda e: e.tensor_copy(lgt, ps_w[:, 0:16]),
                   wait=[ch_lg], sig=[(ch_rt, 1)])
            P.step("vector", lambda e: e.reduce_max(m1c, lgt, axis=AX.X))
            P.step("vector", lambda e: e.tensor_scalar_mul(nm1c, m1c, -1.0),
                   sig=[(ch_rt, 1)])
            P.step("scalar", lambda e: e.activation(
                prb, lgt, AF.Exp, bias=nm1c, accum_out=sumc),
                wait=[ch_rt], sig=[(ch_rt, 1)])
            P.step("vector", lambda e: e.reciprocal(rcol, sumc), wait=[ch_rt])
            P.step("vector", lambda e: e.tensor_scalar(
                prb, prb, rcol, None, op0=ALU.mult))
            P.step("vector", lambda e: e.reduce_max(m1c, prb, axis=AX.X))
            P.step("vector", lambda e: e.tensor_scalar(
                eq1, prb, m1c, None, op0=ALU.is_equal))
            P.step("vector", lambda e: e.tensor_scalar_mul(nm1c, m1c, -1.0))
            P.step("vector", lambda e: e.scalar_tensor_tensor(
                xx2, eq1, nm1c, prb, op0=ALU.mult, op1=ALU.add))
            P.step("vector", lambda e: e.reduce_max(m2c, xx2, axis=AX.X))
            P.step("vector", lambda e: e.tensor_add(denc, m1c, m2c))
            P.step("vector", lambda e: e.tensor_scalar(
                denc, denc, 1.0e-6, None, op0=ALU.add))
            P.step("vector", lambda e: e.reciprocal(denc, denc))
            P.step("vector", lambda e: e.tensor_mul(w1c, m1c, denc))
            P.step("vector", lambda e: e.tensor_mul(w2c, m2c, denc))
            P.step("vector", lambda e: e.tensor_scalar(
                eq2, xx2, m2c, None, op0=ALU.is_equal))
            P.step("vector", lambda e: e.tensor_scalar(
                eq2, eq2, w2c, None, op0=ALU.mult))
            P.step("vector", lambda e: e.scalar_tensor_tensor(
                comb, eq1, w1c, eq2, op0=ALU.mult, op1=ALU.add),
                sig=[(ch_rt, 1)])
            P.step("tensor", lambda e: e.transpose(
                ps_w[0:16, 128:256], comb, id_f32[:, :]),
                wait=[ch_rt, (ch_cmb, ch_cmb.n)], sig=[(ch_rt, 1)])
            P.step("vector", lambda e: e.tensor_copy(
                combT[0:16, 0:128], ps_w[0:16, 128:256]),
                wait=[ch_rt, (ch_cmb, ch_cmb.n)], sig=[(ch_rt, 1), (ch_cmb, 1)])
            P.step("sync", lambda e, t2=tt: e.dma_start(
                out=combT_d[:, t2 * 128:(t2 + 1) * 128], in_=combT[0:16, 0:128]),
                wait=[(ch_cmb, ch_cmb.n)], sig=[(ch_cmb, 16)])
        CMB = ch_cmb.n
        if upto == 81:
            P.replay(blk)
            return nc, P

        NB2 = c.TOK // c.TB
        for tb in range(NB2):
            for half in range(2):
                for ci in range(c.DCH):
                    sg = []
                    if ci == c.DCH - 1 and half == 1:
                        sg = [(ch_res, 1)]
                    P.step("tensor", lambda e, c2=ci, t2=tb, h2=half: e.matmul(
                        ps_o[h2 * 64:(h2 + 1) * 64, 0:c.TB],
                        atslab.rearrange("p (ch r) -> p ch r", r=64)[:, c2, :],
                        h2T.rearrange("p (ch t) -> p ch t", t=c.TOK)[:, c2,
                                      t2 * c.TB:(t2 + 1) * c.TB],
                        start=(c2 == 0), stop=(c2 == c.DCH - 1)),
                        wait=[(ch_cl8, 32), (ch_h2, H2EV), (ch_res, ch_res.n)]
                             if (ci == 0 and half == 0) else (),
                        sig=sg)
            P.step("vector", lambda e, t2=tb: e.tensor_copy(
                resTd[:, t2 * c.TB:(t2 + 1) * c.TB], ps_o[:, 0:c.TB]),
                wait=[ch_res], sig=[(ch_res, 1)], dep=False)
        RESN = ch_res.n
        if upto == 82:
            P.replay(blk)
            return nc, P


        ch_srow = P.chan("p8_srow")
        crow2 = crow[0]
        for rc in range(c.RC):
            P.step("gpsimd", lambda e, e2=rc * 2: e.dma_start(
                out=crow2[0:2, :], in_=combT_d[e2:e2 + 2, :]),
                wait=[(ch_cmb, CMB), (ch_ltp, LTP)] +
                     ([(ch_sst, ch_sst.n)] if rc > 0 else []),
                sig=[(ch_srow, 16)], isdma=True)
            for tb in range(NB2):
                P.step("tensor", lambda e, t2=tb: e.matmul(
                    ps_w[:, 0:c.TB],
                    selm[0:2, :],
                    crow2[0:2, t2 * c.TB:(t2 + 1) * c.TB],
                    start=True, stop=True),
                    wait=[ch_srow, (ch_sst, ch_sst.n)],
                    sig=[(ch_sst, 1)])
                P.step("vector", lambda e, r2=rc, t2=tb: e.tensor_tensor(
                    sstack[:, r2 * c.TOK + t2 * c.TB:r2 * c.TOK + (t2 + 1) * c.TB],
                    ps_w[:, 0:c.TB], resTd[:, t2 * c.TB:(t2 + 1) * c.TB], ALU.mult),
                    wait=[ch_sst, (ch_res, RESN)], sig=[(ch_sst, 1)], dep=False)

        SSTN = ch_sst.n
        if upto == 83:
            P.replay(blk)
            return nc, P


        bsrc = basewT.rearrange("(ch p) e -> p ch e", p=128)
        dsrc = bstk_in.rearrange("(rc p) e -> p rc e", p=128)

        def bslab8_half(slot):
            return bslab8[:, slot * c.DCH * 512:(slot + 1) * c.DCH * 512]

        def dslab8_half(slot):
            return dslab8[:, slot * c.RC * 512:(slot + 1) * c.RC * 512]

        for eb in range(c.EB1):
            sl = eb % 2
            w = [ch_wf[sl]] + ([(ch_ltp, LTP)] if eb < 2 else [])
            segs = max(1, c.DCH // 8)
            per = c.DCH // segs
            for sgi in range(segs):
                P.step("gpsimd", lambda e, s=sgi, c0=eb * 512, t=bslab8_half(sl), p=per: e.dma_start(
                    out=t.rearrange("p (ch k) -> p ch k", k=512)[:, s * p:(s + 1) * p, :],
                    in_=bsrc[:, s * p:(s + 1) * p, c0:c0 + 512]),
                    wait=w if sgi == 0 else (), sig=[(ch_wl[sl], 16)], isdma=True)
            P.step("gpsimd", lambda e, c0=eb * 512, t=dslab8_half(sl): e.dma_start(
                out=t.rearrange("p (rc k) -> p rc k", k=512),
                in_=dsrc[:, :, c0:c0 + 512]),
                sig=[(ch_wl[sl], 16)], isdma=True)
            for tt in range(c.TT):
                pslot = tt % 4
                xsl = tt % 2
                dma("sync", xtl8[xsl], x1d[tt * 128:(tt + 1) * 128, eb * 512:(eb + 1) * 512],
                    wait=[(ch_x1sd[0], X1SD[0]), (ch_x1sd[1], X1SD[1]), (ch_ltp, LTP)] +
                         ([ch_xf[xsl]] if ch_xf[xsl].n > 0 else []),
                    sig=[(ch_xl[xsl], 16)])
                skip_delta = False
                skip_bias = False
                for ci in range(c.DCH):
                    wl = []
                    if ci == 0:
                        wl = [ch_wl[sl], (ch_sst, SSTN), ch_pf[pslot], (ldg, 16)]
                    P.step("tensor", lambda e, p2=pslot, s=sl, c2=ci, t2=tt: e.matmul(
                        ps_mm[p2],
                        h2T.rearrange("p (ch t) -> p ch t", t=c.TOK)[:, c2,
                                      t2 * 128:(t2 + 1) * 128],
                        bslab8_half(s).rearrange("p (ch k) -> p ch k", k=512)[:, c2, :],
                        start=(c2 == 0), stop=(skip_delta and skip_bias and c2 == c.DCH - 1)),
                        wait=wl)
                if not skip_bias:
                    P.step("tensor", lambda e, p2=pslot, e2=eb: e.matmul(
                        ps_mm[p2], onesrow_bf[0:1, :],
                        basebrow[0:1, e2 * 512:(e2 + 1) * 512],
                        start=False, stop=skip_delta))
                sg = [(ch_ps[pslot], 1)]
                if tt == c.TT - 1:
                    sg.append((ch_wf[sl], 1))
                if not skip_delta:
                    for rc in range(c.RC):
                        P.step("tensor", lambda e, p2=pslot, s=sl, r2=rc, t2=tt: e.matmul(
                            ps_mm[p2],
                            sstack.rearrange("p (rc t) -> p rc t", t=c.TOK)[:, r2,
                                             t2 * 128:(t2 + 1) * 128],
                            dslab8_half(s).rearrange("p (rc k) -> p rc k", k=512)[:, r2, :],
                            start=False, stop=(r2 == c.RC - 1)),
                            sig=sg if rc == c.RC - 1 else ())
                else:
                    P.step("tensor", lambda e: e.drain(), sig=sg)
                P.step("vector", lambda e, p2=pslot, xs=xsl: e.tensor_tensor(
                    outs8[xs], ps_mm[p2], xtl8[xs], ALU.add),
                    wait=[ch_ps[pslot], ch_xl[xsl], (ch_out[xsl], ch_out[xsl].n)],
                    sig=[(ch_pf[pslot], 1), (ch_xf[xsl], 1), (ch_ev[xsl], 1)],
                    dep=False)
                P.step("sync", lambda e, t2=tt, e2=eb, xs=xsl: e.dma_start(
                    out=y_out[t2 * 128:(t2 + 1) * 128, e2 * 512:(e2 + 1) * 512],
                    in_=outs8[xs]),
                    wait=[ch_ev[xsl]], sig=[(ch_out[xsl], 16)])

        P.step("sync", None, wait=[(ch_out[0], ch_out[0].n),
                                   (ch_out[1], ch_out[1].n)])

        P.replay(blk)

    return nc, P



def _shards(cfg, inputs):
    c = cfg
    x = inputs["L0_attn_hidden_states"]
    wqkvT = np.ascontiguousarray(np.asarray(inputs["L0_attn_w_qkv"], np.float32).T)
    woutT = np.ascontiguousarray(np.asarray(inputs["L0_attn_w_out"], np.float32).T)
    basewT = np.ascontiguousarray(np.asarray(inputs["L0_ffn_base_w"], np.float32).T)
    AT = np.ascontiguousarray(np.asarray(inputs["L0_ffn_A"], np.float32).T)
    RWT = np.ascontiguousarray(np.asarray(inputs["L0_ffn_router_w"], np.float32).T)
    Bstk = np.ascontiguousarray(
        np.asarray(inputs["L0_ffn_B"], np.float32).transpose(0, 2, 1).reshape(c.E * c.R, c.D))
    baseb = np.ascontiguousarray(np.asarray(inputs["L0_ffn_base_b"], np.float32).reshape(1, c.D))
    qlnw = np.ascontiguousarray(np.asarray(inputs["L0_attn_q_ln_w"], np.float32).reshape(c.DCH, 128).T)
    qlnb = np.ascontiguousarray(np.asarray(inputs["L0_attn_q_ln_b"], np.float32).reshape(c.DCH, 128).T)
    klnw = np.ascontiguousarray(np.asarray(inputs["L0_attn_k_ln_w"], np.float32).reshape(c.DCH, 128).T)
    klnb = np.ascontiguousarray(np.asarray(inputs["L0_attn_k_ln_b"], np.float32).reshape(c.DCH, 128).T)
    slopes = np.asarray(alibi_slopes(c.H), np.float32)

    in_maps = []
    for core in range(c.NC):
        g, r = core // c.G, core % c.G
        xs = np.ascontiguousarray(np.asarray(x[g, r * c.TOK:(r + 1) * c.TOK, :], np.float32))
        sl = np.ascontiguousarray(slopes[c.HL * r:c.HL * (r + 1)].reshape(1, c.HL))
        in_maps.append({
            "x": xs, "wqkvT": wqkvT, "woutT": woutT, "basewT": basewT,
            "AT": AT, "RWT": RWT, "Bstk": Bstk, "baseb": baseb,
            "qlnw": qlnw, "qlnb": qlnb, "klnw": klnw, "klnb": klnb,
            "slopes": sl,
        })
    return in_maps


def kernel(**inputs):
    cfg = Cfg()
    nc, _ = build(cfg)
    in_maps = _shards(cfg, inputs)
    res = run_bass_kernel_spmd(nc, in_maps, core_ids=list(range(cfg.NC)))
    y = np.zeros((cfg.B, cfg.S, cfg.D), np.float32)
    for core in range(cfg.NC):
        g, r = core // cfg.G, core % cfg.G
        y[g, r * cfg.TOK:(r + 1) * cfg.TOK, :] = np.asarray(res.results[core]["out"])
    return y


if __name__ == "__main__":
    tc = Cfg(D=1024, S=512, H=8, R=64, E=16)
    nc, P = build(tc)
    print("instruction counts:", P.counts())
